# version 13
# baseline (speedup 1.0000x reference)
import numpy as np
import ml_dtypes
from contextlib import ExitStack

import concourse.bass as bass
import concourse.mybir as mybir
from concourse.bass_utils import run_bass_kernel_spmd

F32 = mybir.dt.float32
BF16 = mybir.dt.bfloat16
I32 = mybir.dt.int32
AF = mybir.ActivationFunctionType
ALU = mybir.AluOpType
AX = mybir.AxisListType.X
BF = ml_dtypes.bfloat16

D = 1024
T = 8192
NCH = 8
NTILE = 16
NTOK = 2048
EPS = 1e-5
NEG = -1.0e30

ENGINES = ("sync", "gpsimd", "tensor", "vector", "scalar")


def tmax(*tokens):
    best = {}
    for t in tokens:
        if t is None:
            continue
        if isinstance(t, (list, tuple)) and t and isinstance(t[0], (list, tuple)):
            for u in t:
                if u is None:
                    continue
                s, v = u
                if best.get(s, -1) < v:
                    best[s] = v
            continue
        s, v = t
        if best.get(s, -1) < v:
            best[s] = v
    return tuple(best.items())


class Plan:
    def __init__(self):
        self.steps = []
        self.ctr = {}

    def emit(self, engine, fn, waits=(), inc=None, amt=1):
        self.steps.append((engine, fn, tmax(*waits) if waits else (), inc, amt))
        if inc is None:
            return None
        self.ctr[inc] = self.ctr.get(inc, 0) + amt
        return (inc, self.ctr[inc])

    def dma(self, engine, out, in_, waits=(), sem=None):
        assert sem is not None, "every DMA needs a stream semaphore"
        return self.emit(engine, lambda e, o=out, i=in_: e.dma_start(o, i),
                         waits=waits, inc=sem, amt=16)


def build_program(debug=False):
    nc = bass.Bass("TRN2", target_bir_lowering=False, debug=False)

    def din(name, shape, dt=F32):
        return nc.dram_tensor(name, list(shape), dt, kind="ExternalInput").ap()

    xt_in = din("xt_in", (NTILE, 128, NCH * 512))
    xres_in = din("xres_in", (NTOK, D))
    wq_in = din("wq_in", (128, NCH * 256))
    wk_in = din("wk_in", (128, NCH * 256))
    wv_in = din("wv_in", (128, NCH * 256))
    wo_in = din("wo_in", (128, NCH * D))
    w1_in = din("w1_in", (32, 128, NCH * 128))
    w2_in = din("w2_in", (32, 128, NCH * 128))
    h1_in = din("h1_in", (32, 128, 32 * 128))
    h2_in = din("h2_in", (32, 128, 32 * 128))
    w3_in = din("w3_in", (32, 128, D))
    w1s_in = din("w1s_in", (128, 32))
    w2s_in = din("w2s_in", (128, 32))
    h1s_in = din("h1s_in", (128, 32))
    h2s_in = din("h2s_in", (128, 32))
    w3sb_in = din("w3sb_in", (128, D))
    cos_in = din("cos_in", (128, T))
    sin_in = din("sin_in", (128, T))
    rmat_in = din("rmat_in", (128, 128), BF16)
    ident_in = din("ident_in", (128, 128), BF16)
    indic_in = din("indic_in", (128, 33), BF16)
    ones1_in = din("ones1_in", (1, 128))
    sel_lo_in = din("sel_lo_in", (1, 128))
    sel_hi_in = din("sel_hi_in", (1, 128))
    masks_in = din("masks_in", (128, 4 * 1024))
    off_in = din("off_in", (1, 8), I32)

    y_out = nc.dram_tensor("y_out", [NTOK, D], F32, kind="ExternalOutput").ap()
    if debug:
        qT_dbg = nc.dram_tensor("qT_dbg", [128, 2 * T], BF16, kind="ExternalOutput").ap()
        kT_dbg = nc.dram_tensor("kT_dbg", [128, 2 * T], BF16, kind="ExternalOutput").ap()
        v_dbg = nc.dram_tensor("v_dbg", [128, 64 * 130], BF16, kind="ExternalOutput").ap()
        ag_dbg = nc.dram_tensor("ag_dbg", [1024, T], BF16, kind="ExternalOutput").ap()
        x1_dbg = nc.dram_tensor("x1_dbg", [NTOK, D], F32, kind="ExternalOutput").ap()
        u_dbg = nc.dram_tensor("u_dbg", [32, 128, NTOK], BF16, kind="ExternalOutput").ap()
        bias_dbg = nc.dram_tensor("bias_dbg", [128, 64], F32, kind="ExternalOutput").ap()

    ot_dram = [nc.dram_tensor(f"ot_scratch{i}", [128, T], BF16, kind="Internal").ap() for i in range(2)]
    ag_dram = [nc.dram_tensor(f"ag_scratch{i}", [512, T], BF16, kind="Internal").ap() for i in range(2)]
    x1_dram = nc.dram_tensor("x1_scratch", [NTOK, D], F32, kind="Internal").ap()
    u_dram = nc.dram_tensor("u_scratch", [32, 128, NTOK], BF16, kind="Internal").ap()
    w3b_dram = nc.dram_tensor("w3b_scratch", [32, 128, D], BF16, kind="Internal").ap()

    P = Plan()
    ctx = ExitStack()
    sb = lambda n, s, dt: ctx.enter_context(nc.sbuf_tensor(n, list(s), dt))
    attn_ctx = ExitStack()
    sba = lambda n, s, dt: attn_ctx.enter_context(nc.sbuf_tensor(n, list(s), dt))

    wo = sb("wo", (128, NCH * D), BF16)
    ident = sb("ident", (128, 128), BF16)
    w3sb = sb("w3sb", (128, D), F32)
    scal = {n: sb(n, (128, 32), F32) for n in ("w1s", "w2s", "h1s", "h2s")}
    off_sb = sb("off_sb", (1, 8), I32)
    qT = sba("qT", (128, 2 * T), BF16)
    kT = sba("kT", (128, 2 * T), BF16)
    vsb = [sba(f"v{cc}", (128, 64 * 130), BF16) for cc in range(2)]
    wq = sba("wq", (128, NCH * 256), BF16)
    wk = sba("wk", (128, NCH * 256), BF16)
    wv = sba("wv", (128, NCH * 256), BF16)
    rmat = sba("rmat", (128, 128), BF16)
    indic = sba("indic", (128, 33), BF16)
    ones1 = sba("ones1", (1, 128), F32)
    sel_lo = sba("sel_lo", (1, 128), F32)
    sel_hi = sba("sel_hi", (1, 128), F32)
    masks = sba("masks", (128, 4 * 1024), F32)
    qstat = sba("qstat", (1, 64), F32)
    kstat = sba("kstat", (1, 64), F32)
    kmaxr = sba("kmaxr", (1, 4), F32)
    crow = sba("crow", (1, 64), F32)
    crow2 = sba("crow2", (1, 32), F32)
    bias_all = sba("bias_all", (128, 32), F32)

    rgo = ctx.enter_context(nc.gpsimd.register("rgo"))

    for dst, src in ((wq, wq_in), (wk, wk_in), (wv, wv_in), (wo, wo_in)):
        t_cg = P.dma("gpsimd", dst[:, :], src, sem="cg")
    for dst, src in ((rmat, rmat_in), (ident, ident_in), (indic, indic_in),
                     (ones1, ones1_in), (sel_lo, sel_lo_in), (sel_hi, sel_hi_in),
                     (masks, masks_in), (w3sb, w3sb_in), (off_sb, off_in)):
        t_cs = P.dma("sync", dst[:, :], src, sem="cs")
    for n, src in (("w1s", w1s_in), ("w2s", w2s_in), ("h1s", h1s_in), ("h2s", h2s_in)):
        t_cs = P.dma("sync", scal[n][:, :], src, sem="cs")
    t_wq = t_wk = t_wv = t_wo = t_cg
    t_rm = t_id = t_ic = t_o1 = t_sl = t_sh = t_mk = t_w3b = t_off = t_cs
    t_scal = {n: t_cs for n in ("w1s", "w2s", "h1s", "h2s")}
    v3 = [v.rearrange("p (b c) -> p b c", c=130) for v in vsb]
    t_vones = []
    for cc in range(2):
        t_vones.append(P.emit("vector", lambda e, a=v3[cc]: e.memset(a[:, :, 64:65], 1.0), inc="DVE"))
        t_vones.append(P.emit("vector", lambda e, a=v3[cc]: e.memset(a[:, :, 129:130], 1.0), inc="DVE"))

    p1ctx = ExitStack()
    sb1 = lambda n, s, dt: p1ctx.enter_context(nc.sbuf_tensor(n, list(s), dt))
    ps1 = lambda n, s, dt=F32: p1ctx.enter_context(nc.psum_tensor(n, list(s), dt))
    xts = [sb1(f"xt{i}", (128, NCH * 512), BF16) for i in range(2)]
    coss = [sb1(f"cos{i}", (128, 512), F32) for i in range(2)]
    sins = [sb1(f"sin{i}", (128, 512), F32) for i in range(2)]
    k16 = [sb1(f"k16_{i}", (128, 512), BF16) for i in range(2)]
    ropeA = [sb1(f"ropeA{i}", (128, 512), F32) for i in range(2)]
    ropeB = [sb1(f"ropeB{i}", (128, 512), F32) for i in range(2)]
    sqst = [sb1(f"sqst{i}", (128, 512), BF16) for i in range(2)]
    pk = [ps1(f"pk{i}", (128, 512)) for i in range(2)]
    prot = [ps1(f"prot{i}", (128, 512)) for i in range(2)]
    pv = [ps1(f"pv{i}", (128, 256)) for i in range(2)]
    pstat = [ps1(f"pstat{i}", (33, 512)) for i in range(2)]

    t_xt = [None, None]
    xt_users = [[], []]
    t_cos = [None, None]
    t_sin = [None, None]
    tab_users = [[], []]
    pk_free = [[], []]
    k16_free = [None, None]
    ropeA_free = [None, None]
    ropeB_free = [None, None]
    sqst_free = [None, None]
    prot_free = [None, None]
    pv_free = [[], []]
    pstat_free = [[], []]
    t_kT = [None] * NTILE
    t_qT = [None] * NTILE
    t_V = [None] * NTILE
    stat_toks = []

    pcount = 0
    for i in range(NTILE):
        t0 = i * 512
        sl = i % 2
        t_xt[sl] = P.dma("gpsimd", xts[sl][:, :], xt_in[i, :, :], waits=xt_users[sl],
                         sem=f"xt{sl}")
        xt_users[sl] = []
        t_cos[sl] = P.dma("sync", coss[sl][:, :], cos_in[:, t0:t0 + 512], waits=tab_users[sl],
                          sem=f"cos{sl}")
        t_sin[sl] = P.dma("sync", sins[sl][:, :], sin_in[:, t0:t0 + 512], waits=tab_users[sl],
                          sem=f"sin{sl}")
        tab_users[sl] = []
        for cc in range(2):
            for kind in ("k", "q"):
                wsb, t_w = (wk, t_wk) if kind == "k" else (wq, t_wq)
                dest = kT if kind == "k" else qT
                pp = pcount % 2
                pcount += 1

                def mk_proj(e, wsb=wsb, cc=cc, sl=sl, pp=pp):
                    last = None
                    for kc in range(NCH):
                        last = e.matmul(pk[pp][:, :],
                                        wsb[:, kc * 256 + cc * 128:kc * 256 + cc * 128 + 128],
                                        xts[sl][:, kc * 512:kc * 512 + 512],
                                        start=(kc == 0), stop=(kc == NCH - 1))
                    return last
                tok_mm = P.emit("tensor", mk_proj,
                                waits=[t_xt[sl], t_w] + pk_free[pp], inc="PE")
                xt_users[sl].append(tok_mm)
                tok_cp = P.emit("vector",
                                lambda e, pp=pp: e.tensor_copy(k16[pp][:, :], pk[pp][:, :]),
                                waits=[tok_mm, k16_free[pp]], inc="DVE")
                tok_rot = P.emit("tensor",
                                 lambda e, pp=pp: e.matmul(prot[pp][:, :], rmat[:, :], k16[pp][:, :],
                                                           start=True, stop=True),
                                 waits=[tok_cp, t_rm, prot_free[pp]], inc="PE")
                k16_free[pp] = tok_rot
                tok_a = P.emit("vector",
                               lambda e, pp=pp, sl=sl: e.tensor_mul(ropeA[pp][:, :], pk[pp][:, :], coss[sl][:, :]),
                               waits=[tok_mm, t_cos[sl], ropeA_free[pp]], inc="DVE")
                tok_b = P.emit("vector",
                               lambda e, pp=pp, sl=sl: e.tensor_mul(ropeB[pp][:, :], prot[pp][:, :], sins[sl][:, :]),
                               waits=[tok_rot, t_sin[sl], ropeB_free[pp]], inc="DVE")
                prot_free[pp] = tok_b
                dslice = dest[:, cc * T + t0:cc * T + t0 + 512]
                tok_out = P.emit("vector",
                                 lambda e, d=dslice, pp=pp: e.tensor_add(d, ropeA[pp][:, :], ropeB[pp][:, :]),
                                 waits=[tok_a, tok_b], inc="DVE")
                pk_free[pp] = [tok_cp, tok_a]
                ropeA_free[pp] = tok_out
                ropeB_free[pp] = tok_out
                tab_users[sl] += [tok_a, tok_b]
                tok_sq = P.emit("scalar",
                                lambda e, d=dslice, pp=pp: e.activation(sqst[pp][:, :], d, AF.Square),
                                waits=[tok_out, sqst_free[pp]], inc="ACT")
                tok_sm = P.emit("tensor",
                                lambda e, pp=pp: e.matmul(pstat[pp][:, :], indic[:, :], sqst[pp][:, :],
                                                          start=True, stop=True),
                                waits=[tok_sq, t_ic] + pstat_free[pp], inc="PE")
                sqst_free[pp] = tok_sm
                sdst = kstat if kind == "k" else qstat
                r_toks = []
                for ab in range(2):
                    col = (cc * 2 + ab) * 16 + i
                    row = ab * 32
                    r_toks.append(P.emit("vector",
                                         lambda e, sd=sdst, c=col, r=row, pp=pp:
                                         e.reduce_max(sd[0:1, c:c + 1], pstat[pp][r:r + 1, :], axis=AX),
                                         waits=[tok_sm], inc="DVE"))
                pstat_free[pp] = r_toks
                stat_toks += r_toks
                if kind == "k":
                    t_kT[i] = tok_out if t_kT[i] is None else ("DVE", max(t_kT[i][1], tok_out[1]))
                else:
                    t_qT[i] = tok_out if t_qT[i] is None else ("DVE", max(t_qT[i][1], tok_out[1]))
        for st in range(4):
            pp = st % 2
            blk = i * 4 + st

            def mk_vproj(e, sl=sl, st=st, pp=pp):
                last = None
                for kc in range(NCH):
                    last = e.matmul(pv[pp][:, :],
                                    xts[sl][:, kc * 512 + st * 128:kc * 512 + st * 128 + 128],
                                    wv[:, kc * 256:kc * 256 + 256],
                                    start=(kc == 0), stop=(kc == NCH - 1))
                return last
            tok_vm = P.emit("tensor", mk_vproj,
                            waits=[t_xt[sl], t_wv] + pv_free[pp], inc="PE")
            xt_users[sl].append(tok_vm)
            cps = []
            for cc in range(2):
                for ab in range(2):
                    s0c = cc * 128 + ab * 64
                    d0c = blk * 130 + ab * 65
                    cps.append(P.emit("vector",
                                      lambda e, cc=cc, s0c=s0c, d0c=d0c, pp=pp:
                                      e.tensor_copy(vsb[cc][:, d0c:d0c + 64], pv[pp][:, s0c:s0c + 64]),
                                      waits=[tok_vm], inc="DVE"))
            pv_free[pp] = cps
            t_V[i] = cps[-1] if t_V[i] is None else ("DVE", max(t_V[i][1], cps[-1][1]))

    t_allstat = stat_toks
    tok_ks = P.emit("scalar", lambda e: e.activation(kstat[:, :], kstat[:, :], AF.Sqrt),
                    waits=t_allstat, inc="ACT")
    tok_qs = P.emit("scalar", lambda e: e.activation(qstat[:, :], qstat[:, :], AF.Sqrt),
                    waits=t_allstat, inc="ACT")
    tok_km = []
    for hl in range(4):
        tok_km.append(P.emit("vector",
                             lambda e, hl=hl: e.reduce_max(kmaxr[0:1, hl:hl + 1],
                                                           kstat[0:1, hl * 16:(hl + 1) * 16], axis=AX),
                             waits=[tok_ks], inc="DVE"))
    tok_crs = []
    for hl in range(4):
        tok_crs.append(P.emit("vector",
                              lambda e, hl=hl: e.tensor_scalar_mul(crow[0:1, hl * 16:(hl + 1) * 16],
                                                                   qstat[0:1, hl * 16:(hl + 1) * 16],
                                                                   kmaxr[0:1, hl:hl + 1]),
                              waits=[tok_qs] + tok_km, inc="DVE"))
    tok_cr = tok_crs[-1]
    tok_cr2 = P.emit("vector",
                     lambda e: e.tensor_scalar(crow[:, :], crow[:, :], -0.125, -1.0, ALU.mult, ALU.add),
                     waits=tok_crs, inc="DVE")
    cr_ev = bass.AP(crow, 0, [[64, 1], [32, 2], [1, 16]])
    cr_od = bass.AP(crow, 16, [[64, 1], [32, 2], [1, 16]])
    cr2_v = crow2[0:1, :].rearrange("a (c t) -> a c t", c=2)
    tok_pm = P.emit("vector", lambda e: e.tensor_tensor(cr2_v, cr_ev, cr_od, ALU.min),
                    waits=[tok_cr2], inc="DVE")
    tok_bmm = P.emit("tensor", lambda e: e.matmul(pk[0][:, 0:32], ones1[:, :], crow2[:, :],
                                                  start=True, stop=True),
                     waits=[tok_pm, t_o1] + pk_free[0], inc="PE")
    t_bias = P.emit("vector", lambda e: e.tensor_copy(bias_all[:, :], pk[0][:, 0:32]),
                    waits=[tok_bmm], inc="DVE")
    t_p1 = [t_bias] + t_kT + t_qT + t_V + t_vones
    w3cast_toks = [P.dma("gpsimd", w3b_dram[fc, :, :], w3_in[fc, :, :], sem="w3c")
                   for fc in range(32)]
    if debug:
        P.dma("sync", qT_dbg, qT[:, :], waits=t_p1, sem="dbg")
        P.dma("sync", kT_dbg, kT[:, :], waits=t_p1, sem="dbg")
        P.dma("sync", v_dbg, vsb[0][:, :], waits=t_p1, sem="dbg")
        P.dma("sync", bias_dbg, bias_all[:, :], waits=t_p1, sem="dbg")
    p1ctx.close()

    p2ctx = ExitStack()
    sb2 = lambda n, s, dt: p2ctx.enter_context(nc.sbuf_tensor(n, list(s), dt))
    ps2 = lambda n, s, dt=F32: p2ctx.enter_context(nc.psum_tensor(n, list(s), dt))
    pbuf2 = sb2("pbuf2", (128, 4 * 1024), BF16)
    otsb = [sb2(f"otsb{i}", (128, 512), BF16) for i in range(2)]
    rrow = [sb2(f"rrow{h}", (1, 512), F32) for h in range(2)]
    rbc_sb = sb2("rbc_sb", (128, 512), F32)
    pscore2 = [ps2(f"psc2_{p}", (128, 1024)) for p in range(3)]
    pav = [ps2(f"pav{h}", (65, 512)) for h in range(2)]

    exp_tok = {}
    av_tok = {}
    evac_tok = [None, None]
    pav_free = []
    last_rc = [None]
    rbc_free = [[]]
    ot_dma = []
    cc_done = [None, None]

    for cc in range(2):
        for ttb in range(NTILE):
            nchunk = 4 * (ttb + 1)
            etoks_arr = [None] * nchunk

            def emit_sc_exp(ci, ttb=ttb, cc=cc, nchunk=nchunk, etoks_arr=etoks_arr):
                s0 = ci * 128
                par = ci % 3
                par4 = ci % 4
                diag = ci >= 4 * ttb
                sc_toks = []
                for ab in range(2):
                    lhs = kT[ab * 64:(ab + 1) * 64, cc * T + s0:cc * T + s0 + 128]
                    rhs = qT[ab * 64:(ab + 1) * 64, cc * T + ttb * 512:cc * T + (ttb + 1) * 512]
                    sc_toks.append(P.emit("tensor",
                                          lambda e, ab=ab, par=par, l=lhs, r=rhs:
                                          e.matmul(pscore2[par][:, ab * 512:ab * 512 + 512], l, r,
                                                   start=True, stop=True, tile_position=(ab * 64, 0)),
                                          waits=[t_kT[ci // 4], t_qT[ttb], exp_tok.get(par),
                                                 last_rc[0] if (ci == 0 and par == 0) else None] + t_p1,
                                          inc="PE"))
                mtk = None
                if diag:
                    r = (ci - 4 * ttb) * 128
                    mtk = P.emit("vector",
                                 lambda e, par=par, r=r:
                                 e.tensor_add(pscore2[par][:, :], pscore2[par][:, :],
                                              masks[:, (r // 128) * 1024:(r // 128) * 1024 + 1024]),
                                 waits=sc_toks + [t_mk], inc="DVE")
                bc = cc * 16 + ttb
                etok = P.emit("scalar",
                              lambda e, par=par, par4=par4, bc=bc:
                              e.activation(pbuf2[:, par4 * 1024:par4 * 1024 + 1024],
                                           pscore2[par][:, :], AF.Exp,
                                           bias=bias_all[:, bc:bc + 1], scale=0.125),
                              waits=([mtk] if mtk else sc_toks) + [t_bias,
                                    av_tok.get((0, par4)), av_tok.get((1, par4))],
                              inc="ACT")
                exp_tok[par] = etok
                etoks_arr[ci] = etok

            def emit_av(ci, ttb=ttb, cc=cc, nchunk=nchunk, etoks_arr=etoks_arr):
                par4 = ci % 4
                for ab in range(2):
                    lhsv = vsb[cc][:, ci * 130 + ab * 65:ci * 130 + ab * 65 + 65]
                    first, last = ci == 0, ci == nchunk - 1
                    av_tok[(ab, par4)] = P.emit("tensor",
                                                lambda e, ab=ab, par4=par4, lv=lhsv, f=first, lst=last:
                                                e.matmul(pav[ab][:, :], lv,
                                                         pbuf2[:, par4 * 1024 + ab * 512:par4 * 1024 + ab * 512 + 512],
                                                         start=f, stop=lst),
                                                waits=[etoks_arr[ci], t_V[ci // 4]] +
                                                      (pav_free if first else []),
                                                inc="PE")

            emit_sc_exp(0)
            if nchunk > 1:
                emit_sc_exp(1)
            if nchunk > 2:
                emit_sc_exp(2)
            for ci in range(nchunk):
                if ci + 3 < nchunk:
                    emit_sc_exp(ci + 3)
                emit_av(ci)

            rtoks = []
            for ab in range(2):
                rtoks.append(P.emit("vector",
                                    lambda e, ab=ab: e.reciprocal(rrow[ab][:, :], pav[ab][64:65, :]),
                                    waits=[av_tok.get((ab, p4)) for p4 in range(4)], inc="DVE"))

            def mk_rbc(e):
                e.matmul(pscore2[0][:, 0:512], sel_lo[:, :], rrow[0][:, :], start=True, stop=False)
                return e.matmul(pscore2[0][:, 0:512], sel_hi[:, :], rrow[1][:, :], start=False, stop=True)
            tok_rb = P.emit("tensor", mk_rbc,
                            waits=[rtoks[0], rtoks[1], t_sl, t_sh, exp_tok.get(0)], inc="PE")
            tok_rc = P.emit("vector", lambda e: e.tensor_copy(rbc_sb[:, :], pscore2[0][:, 0:512]),
                            waits=[tok_rb] + rbc_free[0], inc="DVE")
            last_rc[0] = tok_rc
            osl = ttb % 2
            ev = []
            ev.append(P.emit("vector",
                             lambda e, osl=osl: e.tensor_mul(otsb[osl][0:64, :], pav[0][0:64, :], rbc_sb[0:64, :]),
                             waits=[tok_rc, evac_tok[osl]], inc="DVE"))
            ev.append(P.emit("vector",
                             lambda e, osl=osl: e.tensor_mul(otsb[osl][64:128, :], pav[1][0:64, :], rbc_sb[64:128, :]),
                             waits=[tok_rc, evac_tok[osl]], inc="DVE"))
            pav_free = ev + rtoks
            rbc_free[0] = ev
            evac_tok[osl] = P.dma("sync", ot_dram[cc][:, ttb * 512:(ttb + 1) * 512],
                                  otsb[osl][:, :], waits=ev, sem="ots")
            ot_dma.append(evac_tok[osl])
        cc_done[cc] = list(ot_dma)
        ot_dma = []

    for cc in range(2):
        t_cc = P.emit("gpsimd",
                      lambda e, cc=cc: e.collective_compute("AllGather", ALU.bypass,
                                                            replica_groups=[[0, 1, 2, 3], [4, 5, 6, 7]],
                                                            ins=[ot_dram[cc][:, :].opt()],
                                                            outs=[ag_dram[cc][:, :].opt()]),
                      waits=cc_done[cc], inc="CC")
    if debug:
        P.dma("sync", ag_dbg, ag_dram[:, :], waits=[t_cc], sem="dbg")
    p2ctx.close()
    attn_ctx.close()

    x1T_ctx = ExitStack()
    x1T = x1T_ctx.enter_context(nc.sbuf_tensor("x1T", [128, NCH * NTOK], BF16))
    pwctx = ExitStack()
    sbw = lambda n, s, dt: pwctx.enter_context(nc.sbuf_tensor(n, list(s), dt))
    psw = lambda n, s, dt=F32: pwctx.enter_context(nc.psum_tensor(n, list(s), dt))
    go = sbw("go", (128, NCH * NTOK), BF16)
    xres = [sbw(f"xres{i}", (128, D), F32) for i in range(2)]
    x1st = [sbw(f"x1st{i}", (128, D), F32) for i in range(2)]
    x116 = [sbw(f"x116{i}", (128, D), BF16) for i in range(2)]
    st12 = [sbw(f"st12_{i}", (128, 12), F32) for i in range(2)]
    st2 = [sbw(f"st2_{i}", (128, 2), F32) for i in range(2)]
    stdv = [sbw(f"stdv{i}", (128, 1), F32) for i in range(2)]
    rstd = [sbw(f"rstd{i}", (128, 1), F32) for i in range(2)]
    po = [[psw(f"po{h}_{p}", (128, 512)) for p in range(2)] for h in range(2)]
    ptr = [psw(f"ptr{p}", (128, 1024), BF16) for p in range(2)]

    go_toks = []
    for dc in range(NCH):
        def mk_go(e, dc=dc):
            e.reg_load(rgo, off_sb[0:1, dc:dc + 1])
            return e.dma_start(go[:, dc * NTOK:(dc + 1) * NTOK],
                               bass.AP(ag_dram[dc % 2].tensor, rgo, [[T, 128], [1, NTOK]]))
        go_toks.append(P.emit("gpsimd", mk_go, waits=[("CC", 2), t_off], inc="go", amt=16))

    t_xres = [None, None]
    xres_users = [[], []]
    po_free = [[[], []], [[], []]]
    x1st_free = [None, None]
    x116_free = [None, None]
    ptr_free = [[], []]
    x1T_toks = []
    x1d_toks = []
    for tt in range(NTILE):
        sl = tt % 2
        t_xres[sl] = P.dma("sync", xres[sl][:, :], xres_in[tt * 128:(tt + 1) * 128, :],
                           waits=xres_users[sl], sem=f"xr{sl}")
        xres_users[sl] = []
        mm_toks = []
        for eh in range(2):
            def mk_wo(e, tt=tt, eh=eh, sl=sl):
                last = None
                for dc in range(NCH):
                    last = e.matmul(po[eh][sl][:, :],
                                    go[:, dc * NTOK + tt * 128:dc * NTOK + tt * 128 + 128],
                                    wo[:, dc * D + eh * 512:dc * D + eh * 512 + 512],
                                    start=(dc == 0), stop=(dc == NCH - 1))
                return last
            mm_toks.append(P.emit("tensor", mk_wo,
                                  waits=go_toks + [t_wo] + po_free[eh][sl], inc="PE"))
        add_toks = []
        for eh in range(2):
            add_toks.append(P.emit("vector",
                                   lambda e, eh=eh, sl=sl:
                                   e.tensor_add(x1st[sl][:, eh * 512:eh * 512 + 512],
                                                po[eh][sl][:, :], xres[sl][:, eh * 512:eh * 512 + 512]),
                                   waits=[mm_toks[eh], t_xres[sl], x1st_free[sl]], inc="DVE"))
        xres_users[sl] += add_toks
        for eh in range(2):
            po_free[eh][sl] = [add_toks[eh]]
        bn_toks = []
        for eh in range(2):
            bn_toks.append(P.emit("vector",
                                  lambda e, eh=eh, sl=sl:
                                  e.bn_stats(st12[sl][:, eh * 6:eh * 6 + 6], x1st[sl][:, eh * 512:eh * 512 + 512]),
                                  waits=[add_toks[eh]], inc="DVE"))
        tok_ag = P.emit("vector", lambda e, sl=sl: e.bn_aggr(st2[sl][:, :], st12[sl][:, :]),
                        waits=bn_toks, inc="DVE")
        tok_ve = P.emit("vector", lambda e, sl=sl: e.tensor_scalar_add(stdv[sl][:, :], st2[sl][:, 1:2], EPS),
                        waits=[tok_ag], inc="DVE")
        tok_sd = P.emit("scalar", lambda e, sl=sl: e.activation(stdv[sl][:, :], stdv[sl][:, :], AF.Sqrt),
                        waits=[tok_ve], inc="ACT")
        tok_rs = P.emit("vector", lambda e, sl=sl: e.reciprocal(rstd[sl][:, :], stdv[sl][:, :]),
                        waits=[tok_sd], inc="DVE")
        tok_x1 = P.emit("vector",
                        lambda e, sl=sl: e.tensor_scalar(x1st[sl][:, :], x1st[sl][:, :],
                                                         st2[sl][:, 0:1], rstd[sl][:, :],
                                                         ALU.subtract, ALU.mult),
                        waits=[tok_rs], inc="DVE")
        tok_xd = P.dma("sync", x1_dram[tt * 128:(tt + 1) * 128, :], x1st[sl][:, :], waits=[tok_x1],
                       sem="x1s")
        x1d_toks.append(tok_xd)
        tok_c16 = P.emit("vector", lambda e, sl=sl: e.tensor_copy(x116[sl][:, :], x1st[sl][:, :]),
                         waits=[tok_x1, x116_free[sl]], inc="DVE")
        x1st_free[sl] = [tok_xd, tok_c16]

        def mk_tr(e, sl=sl):
            last = None
            for ec in range(NCH):
                last = e.transpose(ptr[sl][:, ec * 128:ec * 128 + 128],
                                   x116[sl][:, ec * 128:ec * 128 + 128], ident[:, :])
            return last
        tok_tr = P.emit("tensor", mk_tr, waits=[tok_c16, t_id] + ptr_free[sl], inc="PE")
        x116_free[sl] = tok_tr
        evs = []
        for ec in range(NCH):
            evs.append(P.emit("vector",
                              lambda e, ec=ec, tt=tt, sl=sl:
                              e.tensor_copy(x1T[:, ec * NTOK + tt * 128:ec * NTOK + tt * 128 + 128],
                                            ptr[sl][:, ec * 128:ec * 128 + 128]),
                              waits=[tok_tr], inc="DVE"))
        ptr_free[sl] = evs
        x1T_toks += evs
    t_pw_pe = P.ctr["PE"]
    pwctx.close()

    f1ctx = ExitStack()
    sbf = lambda n, s, dt: f1ctx.enter_context(nc.sbuf_tensor(n, list(s), dt))
    psf = lambda n, s, dt=F32: f1ctx.enter_context(nc.psum_tensor(n, list(s), dt))
    w1c = [sbf(f"w1c{i}", (128, NCH * 128), BF16) for i in range(2)]
    w2c = [sbf(f"w2c{i}", (128, NCH * 128), BF16) for i in range(2)]
    asbuf = [sbf(f"asb{i}", (128, 512), BF16) for i in range(2)]
    u16 = [sbf(f"u16_{i}", (128, 512), BF16) for i in range(2)]
    pu1 = [psf(f"pu1_{i}", (128, 512)) for i in range(2)]
    pu2 = [psf(f"pu2_{i}", (128, 512)) for i in range(2)]

    t_pw = [("PE", t_pw_pe)] + x1d_toks[-2:] + x1T_toks[-2:]
    w1_tok = [None, None]
    w2_tok = [None, None]
    w_users = [[], []]
    pu1_free = [[], []]
    pu2_free = [[], []]
    a_free = [None, None]
    u16_free = [None, None]
    u_dma = []
    for mc in range(32):
        sl = mc % 2
        w1_tok[sl] = P.dma("gpsimd", w1c[sl][:, :], w1_in[mc, :, :], waits=w_users[sl] + t_pw,
                           sem=f"w1c{sl}")
        w2_tok[sl] = P.dma("gpsimd", w2c[sl][:, :], w2_in[mc, :, :], waits=w_users[sl] + t_pw,
                           sem=f"w2c{sl}")
        w_users[sl] = []
        for tq in range(4):
            pp = tq % 2

            def mk_u(e, sl=sl, tq=tq, pp=pp, wsel=0):
                wc = w1c[sl] if wsel == 0 else w2c[sl]
                pu = pu1[pp] if wsel == 0 else pu2[pp]
                last = None
                for kc in range(NCH):
                    last = e.matmul(pu[:, :], wc[:, kc * 128:kc * 128 + 128],
                                    x1T[:, kc * NTOK + tq * 512:kc * NTOK + tq * 512 + 512],
                                    start=(kc == 0), stop=(kc == NCH - 1))
                return last
            tok_m1 = P.emit("tensor", lambda e, f=mk_u: f(e, wsel=0),
                            waits=[w1_tok[sl]] + x1T_toks + pu1_free[pp], inc="PE")
            tok_m2 = P.emit("tensor", lambda e, f=mk_u: f(e, wsel=1),
                            waits=[w2_tok[sl]] + pu2_free[pp], inc="PE")
            w_users[sl] += [tok_m1, tok_m2]
            tok_si = P.emit("scalar",
                            lambda e, pp=pp, mc=mc: e.activation(asbuf[pp][:, :], pu1[pp][:, :], AF.Silu,
                                                                 scale=scal["w1s"][:, mc:mc + 1]),
                            waits=[tok_m1, t_scal["w1s"], a_free[pp]], inc="ACT")
            pu1_free[pp] = [tok_si]
            tok_u = P.emit("vector",
                           lambda e, pp=pp, mc=mc: e.scalar_tensor_tensor(u16[pp][:, :], pu2[pp][:, :],
                                                                          scal["w2s"][:, mc:mc + 1],
                                                                          asbuf[pp][:, :],
                                                                          ALU.mult, ALU.mult),
                           waits=[tok_m2, tok_si, t_scal["w2s"], u16_free[pp]], inc="DVE")
            pu2_free[pp] = [tok_u]
            a_free[pp] = tok_u
            tok_ud = P.dma("sync", u_dram[mc, :, tq * 512:(tq + 1) * 512], u16[pp][:, :], waits=[tok_u],
                           sem="u16")
            u16_free[pp] = tok_ud
            u_dma.append(tok_ud)
    t_f1_pe = P.ctr["PE"]
    if debug:
        P.dma("sync", x1_dbg, x1_dram[:, :], waits=x1d_toks, sem="dbg")
        P.dma("sync", u_dbg, u_dram[:, :, :], waits=u_dma, sem="dbg")
    f1ctx.close()
    x1T_ctx.close()

    f2ctx = ExitStack()
    sbg = lambda n, s, dt: f2ctx.enter_context(nc.sbuf_tensor(n, list(s), dt))
    psg = lambda n, s, dt=F32: f2ctx.enter_context(nc.psum_tensor(n, list(s), dt))
    u2sb = sbg("u2sb", (128, 32 * 1024), BF16)
    big2 = sbg("big2", (128, 32 * 1024), BF16)
    hc = [sbg(f"hc{i}", (128, 32 * 128), BF16) for i in range(2)]
    yst = [sbg(f"yst{i}", (128, D), F32) for i in range(2)]
    x1r = [sbg(f"x1r{i}", (128, D), F32) for i in range(2)]
    outs = [sbg(f"outs{i}", (128, D), F32) for i in range(2)]
    st12b = [sbg(f"st12b{i}", (128, 12), F32) for i in range(2)]
    st2b = [sbg(f"st2b{i}", (128, 2), F32) for i in range(2)]
    stdb = [sbg(f"stdb{i}", (128, 1), F32) for i in range(2)]
    rstb = [sbg(f"rstb{i}", (128, 1), F32) for i in range(2)]
    ph = [psg(f"ph{i}", (128, 512)) for i in range(2)]
    py = [[psg(f"py{h}_{p}", (128, 512)) for p in range(2)] for h in range(2)]

    t_f1 = [("PE", t_f1_pe)] + u_dma[-2:]
    y_dma = []
    u2_users_prev = []
    for th in range(2):
        u_toks = []
        for fc in range(32):
            u_toks.append(P.dma("sync", u2sb[:, fc * 1024:(fc + 1) * 1024],
                                u_dram[fc, :, th * 1024:(th + 1) * 1024],
                                waits=u_dma + t_f1 + u2_users_prev, sem="ut"))
        hc_tok = [None, None]
        hc_users = [[], []]
        ph_free = [[], []]
        h1_acts = []
        for mc in range(32):
            sl = mc % 2
            hc_tok[sl] = P.dma("gpsimd", hc[sl][:, :], h1_in[mc, :, :], waits=hc_users[sl] + t_f1,
                               sem=f"hc{sl}")
            hc_users[sl] = []
            for ts in range(2):
                pp = ts % 2

                def mk_h(e, sl=sl, ts=ts, pp=pp, src=u2sb):
                    last = None
                    for kc in range(32):
                        last = e.matmul(ph[pp][:, :], hc[sl][:, kc * 128:kc * 128 + 128],
                                        src[:, kc * 1024 + ts * 512:kc * 1024 + ts * 512 + 512],
                                        start=(kc == 0), stop=(kc == 31))
                    return last
                tok_hm = P.emit("tensor", mk_h,
                                waits=[hc_tok[sl]] + u_toks + ph_free[pp], inc="PE")
                hc_users[sl].append(tok_hm)
                tok_ha = P.emit("scalar",
                                lambda e, mc=mc, ts=ts, pp=pp:
                                e.activation(big2[:, mc * 1024 + ts * 512:mc * 1024 + ts * 512 + 512],
                                             ph[pp][:, :], AF.Gelu, scale=scal["h1s"][:, mc:mc + 1]),
                                waits=[tok_hm, t_scal["h1s"]], inc="ACT")
                ph_free[pp] = [tok_ha]
                h1_acts.append(tok_ha)
        h2_acts = []
        h1_mm_done = ("PE", P.ctr["PE"])
        for mc in range(32):
            sl = mc % 2
            hc_tok[sl] = P.dma("gpsimd", hc[sl][:, :], h2_in[mc, :, :], waits=hc_users[sl],
                               sem=f"hc{sl}")
            hc_users[sl] = []
            for ts in range(2):
                pp = ts % 2

                def mk_h2(e, sl=sl, ts=ts, pp=pp):
                    last = None
                    for kc in range(32):
                        last = e.matmul(ph[pp][:, :], hc[sl][:, kc * 128:kc * 128 + 128],
                                        big2[:, kc * 1024 + ts * 512:kc * 1024 + ts * 512 + 512],
                                        start=(kc == 0), stop=(kc == 31))
                    return last
                tok_hm = P.emit("tensor", mk_h2,
                                waits=[hc_tok[sl]] + h1_acts + ph_free[pp], inc="PE")
                hc_users[sl].append(tok_hm)
                tok_ha = P.emit("scalar",
                                lambda e, mc=mc, ts=ts, pp=pp:
                                e.activation(u2sb[:, mc * 1024 + ts * 512:mc * 1024 + ts * 512 + 512],
                                             ph[pp][:, :], AF.Gelu, scale=scal["h2s"][:, mc:mc + 1]),
                                waits=[tok_hm, t_scal["h2s"], h1_mm_done], inc="ACT")
                ph_free[pp] = [tok_ha]
                h2_acts.append(tok_ha)
        h2_mm_done = ("PE", P.ctr["PE"])
        w3_toks = []
        for fc in range(32):
            w3_toks.append(P.dma("sync", big2[:, fc * 1024:(fc + 1) * 1024],
                                 w3b_dram[fc, :, :], waits=[h2_mm_done] + w3cast_toks[-1:], sem="w3s"))
        x1r_tok = [None, None]
        x1r_users = [[], []]
        py_free = [[[], []], [[], []]]
        yst_free = [None, None]
        out_free = [None, None]
        u2_users_prev = []
        for tt in range(8):
            sl = tt % 2
            gt = th * 8 + tt
            x1r_tok[sl] = P.dma("sync", x1r[sl][:, :], x1_dram[gt * 128:(gt + 1) * 128, :],
                                waits=x1r_users[sl] + [x1d_toks[-1]], sem=f"x1r{sl}")
            x1r_users[sl] = []
            mm_toks = []
            for eh in range(2):
                def mk_y(e, tt=tt, eh=eh, sl=sl):
                    last = None
                    for fc in range(32):
                        last = e.matmul(py[eh][sl][:, :],
                                        u2sb[:, fc * 1024 + tt * 128:fc * 1024 + tt * 128 + 128],
                                        big2[:, fc * 1024 + eh * 512:fc * 1024 + eh * 512 + 512],
                                        start=(fc == 0), stop=(fc == 31))
                    return last
                mm_toks.append(P.emit("tensor", mk_y,
                                      waits=w3_toks + h2_acts + py_free[eh][sl], inc="PE"))
            u2_users_prev += mm_toks
            sc_toks2 = []
            for eh in range(2):
                sc_toks2.append(P.emit("vector",
                                       lambda e, eh=eh, sl=sl:
                                       e.tensor_mul(yst[sl][:, eh * 512:eh * 512 + 512],
                                                    py[eh][sl][:, :], w3sb[:, eh * 512:eh * 512 + 512]),
                                       waits=[mm_toks[eh], t_w3b, yst_free[sl]], inc="DVE"))
                py_free[eh][sl] = [sc_toks2[eh]]
            tok_add = P.emit("vector",
                             lambda e, sl=sl: e.tensor_add(yst[sl][:, :], yst[sl][:, :], x1r[sl][:, :]),
                             waits=sc_toks2 + [x1r_tok[sl]], inc="DVE")
            x1r_users[sl].append(tok_add)
            bt = []
            for eh in range(2):
                bt.append(P.emit("vector",
                                 lambda e, eh=eh, sl=sl:
                                 e.bn_stats(st12b[sl][:, eh * 6:eh * 6 + 6], yst[sl][:, eh * 512:eh * 512 + 512]),
                                 waits=[tok_add], inc="DVE"))
            tok_ag2 = P.emit("vector", lambda e, sl=sl: e.bn_aggr(st2b[sl][:, :], st12b[sl][:, :]),
                             waits=bt, inc="DVE")
            tok_ve2 = P.emit("vector", lambda e, sl=sl: e.tensor_scalar_add(stdb[sl][:, :], st2b[sl][:, 1:2], EPS),
                             waits=[tok_ag2], inc="DVE")
            tok_sd2 = P.emit("scalar", lambda e, sl=sl: e.activation(stdb[sl][:, :], stdb[sl][:, :], AF.Sqrt),
                             waits=[tok_ve2], inc="ACT")
            tok_rs2 = P.emit("vector", lambda e, sl=sl: e.reciprocal(rstb[sl][:, :], stdb[sl][:, :]),
                             waits=[tok_sd2], inc="DVE")
            tok_o = P.emit("vector",
                           lambda e, sl=sl: e.tensor_scalar(outs[sl][:, :], yst[sl][:, :],
                                                            st2b[sl][:, 0:1], rstb[sl][:, :],
                                                            ALU.subtract, ALU.mult),
                           waits=[tok_rs2, out_free[sl]], inc="DVE")
            yst_free[sl] = tok_o
            tok_yd = P.dma("sync", y_out[gt * 128:(gt + 1) * 128, :], outs[sl][:, :], waits=[tok_o],
                           sem="ys")
            out_free[sl] = tok_yd
            y_dma.append(tok_yd)
    f2ctx.close()

    fence = [(k, v) for k, v in P.ctr.items()]
    P.emit("gpsimd", lambda e: e.nop(), waits=fence)

    sems = {}
    for s in list(P.ctr.keys()):
        sems[s] = ctx.enter_context(nc.semaphore(s))
    with nc.Block() as block:
        for en in ENGINES:
            def body(eng, en=en):
                water = {}
                for (engine, fn, waits, inc, amt) in P.steps:
                    if engine != en:
                        continue
                    for (s, v) in waits:
                        if water.get(s, 0) >= v:
                            continue
                        eng.wait_ge(sems[s], v)
                        water[s] = v
                    ins = fn(eng)
                    if inc is not None:
                        ins.then_inc(sems[inc], amt)
            getattr(block, en)(body)
    ctx.close()
    return nc


def make_inputs(x_full, Wq, Wk, Wv, Wo, W1, w1s, W2, w2s, H1, h1s, H2, h2s, W3, w3s):
    f32 = np.float32
    xt_b = []
    for b in range(2):
        xb = np.asarray(x_full[b], dtype=f32)
        xt_b.append(np.ascontiguousarray(
            xb.reshape(NTILE, 512, NCH, 128).transpose(0, 3, 2, 1).reshape(NTILE, 128, NCH * 512)))

    def quad_w(W, q):
        return np.ascontiguousarray(
            np.asarray(W[256 * q:256 * (q + 1), :], dtype=f32)
            .reshape(256, NCH, 128).transpose(2, 1, 0).reshape(128, NCH * 256))

    wo_h = np.ascontiguousarray(
        np.asarray(Wo, dtype=f32).T.reshape(NCH, 128, D).transpose(1, 0, 2).reshape(128, NCH * D))
    w1_h = np.ascontiguousarray(
        np.asarray(W1, dtype=f32).reshape(32, 128, NCH, 128).transpose(0, 3, 2, 1).reshape(32, 128, NCH * 128))
    w2_h = np.ascontiguousarray(
        np.asarray(W2, dtype=f32).reshape(32, 128, NCH, 128).transpose(0, 3, 2, 1).reshape(32, 128, NCH * 128))
    h1_h = np.ascontiguousarray(
        np.asarray(H1, dtype=f32).reshape(32, 128, 32, 128).transpose(0, 3, 2, 1).reshape(32, 128, 32 * 128))
    h2_h = np.ascontiguousarray(
        np.asarray(H2, dtype=f32).reshape(32, 128, 32, 128).transpose(0, 3, 2, 1).reshape(32, 128, 32 * 128))
    w3_h = np.ascontiguousarray(np.asarray(W3, dtype=f32).T.reshape(32, 128, D))
    scl = {k: np.ascontiguousarray(np.asarray(v, dtype=f32).reshape(32, 128).T)
           for k, v in (("w1s", w1s), ("w2s", w2s), ("h1s", h1s), ("h2s", h2s))}
    w3sb_h = np.ascontiguousarray(np.broadcast_to(np.asarray(w3s, dtype=f32)[None, :], (128, D)))

    j = np.arange(32, dtype=np.float64)
    inv_freq = (1.0 / 10000.0 ** (j / 32.0))
    pos = np.arange(T, dtype=np.float64)
    ang = pos[None, :] * inv_freq[:, None]
    cosr = np.cos(ang).astype(f32)
    sinr = np.sin(ang).astype(f32)
    cos_h = np.ascontiguousarray(np.tile(cosr, (4, 1)))
    sin_half = np.concatenate([-sinr, sinr], axis=0)
    sin_h = np.ascontiguousarray(np.tile(sin_half, (2, 1)))
    Rm = np.zeros((128, 128), dtype=BF)
    for blk in range(2):
        for d in range(32):
            Rm[blk * 64 + d, blk * 64 + 32 + d] = 1
            Rm[blk * 64 + 32 + d, blk * 64 + d] = 1
    ident_h = np.eye(128, dtype=BF)
    indic_h = np.zeros((128, 33), dtype=BF)
    indic_h[0:64, 0] = 1
    indic_h[64:128, 32] = 1
    ones1_h = np.ones((1, 128), dtype=f32)
    sel_lo_h = np.zeros((1, 128), dtype=f32)
    sel_lo_h[0, 0:64] = 1
    sel_hi_h = np.zeros((1, 128), dtype=f32)
    sel_hi_h[0, 64:128] = 1
    masks_h = np.zeros((128, 4, 2, 512), dtype=f32)
    jj = np.arange(512)
    ii = np.arange(128)
    for rb in range(4):
        pat = np.where(jj[None, :] < rb * 128 + ii[:, None], NEG, 0.0)
        masks_h[:, rb, 0, :] = pat
        masks_h[:, rb, 1, :] = pat
    masks_h = np.ascontiguousarray(masks_h.reshape(128, 4 * 1024))

    in_maps = []
    for core in range(8):
        b, q = core // 4, core % 4
        tok0 = q * NTOK
        off_h = np.asarray([[tok0 + (dc // 2) * 128 * T for dc in range(NCH)]], dtype=np.int32)
        im = {
            "xt_in": xt_b[b],
            "xres_in": np.ascontiguousarray(np.asarray(x_full[b][tok0:tok0 + NTOK, :], dtype=f32)),
            "wq_in": quad_w(Wq, q), "wk_in": quad_w(Wk, q), "wv_in": quad_w(Wv, q),
            "wo_in": wo_h, "w1_in": w1_h, "w2_in": w2_h,
            "h1_in": h1_h, "h2_in": h2_h, "w3_in": w3_h,
            "w1s_in": scl["w1s"], "w2s_in": scl["w2s"],
            "h1s_in": scl["h1s"], "h2s_in": scl["h2s"], "w3sb_in": w3sb_h,
            "cos_in": cos_h, "sin_in": sin_h,
            "rmat_in": Rm, "ident_in": ident_h, "indic_in": indic_h,
            "ones1_in": ones1_h, "sel_lo_in": sel_lo_h, "sel_hi_in": sel_hi_h,
            "masks_in": masks_h, "off_in": off_h,
        }
        in_maps.append(im)
    return in_maps


def kernel(L0_attn_x, L0_attn_Wq, L0_attn_Wk, L0_attn_Wv, L0_attn_Wo,
           L0_ffn_w1_w, L0_ffn_w1_s, L0_ffn_w2_w, L0_ffn_w2_s,
           L0_ffn_h1_w, L0_ffn_h1_s, L0_ffn_h2_w, L0_ffn_h2_s,
           L0_ffn_w3_w, L0_ffn_w3_s):
    x = np.asarray(L0_attn_x, dtype=np.float32)
    in_maps = make_inputs(x, L0_attn_Wq, L0_attn_Wk, L0_attn_Wv, L0_attn_Wo,
                          L0_ffn_w1_w, L0_ffn_w1_s, L0_ffn_w2_w, L0_ffn_w2_s,
                          L0_ffn_h1_w, L0_ffn_h1_s, L0_ffn_h2_w, L0_ffn_h2_s,
                          L0_ffn_w3_w, L0_ffn_w3_s)
    nc = build_program()
    res = run_bass_kernel_spmd(nc, in_maps, core_ids=list(range(8)))
    out = np.empty((2, T, D), dtype=np.float32)
    for core in range(8):
        b, q = core // 4, core % 4
        out[b, q * NTOK:(q + 1) * NTOK, :] = res.results[core]["y_out"]
    return out


# version 14
# speedup vs baseline: 1.0194x; 1.0194x over previous
import numpy as np
import ml_dtypes
from contextlib import ExitStack

import concourse.bass as bass
import concourse.mybir as mybir
from concourse.bass_utils import run_bass_kernel_spmd

F32 = mybir.dt.float32
BF16 = mybir.dt.bfloat16
I32 = mybir.dt.int32
AF = mybir.ActivationFunctionType
ALU = mybir.AluOpType
AX = mybir.AxisListType.X
BF = ml_dtypes.bfloat16

D = 1024
T = 8192
NCH = 8
NTILE = 16
NTOK = 2048
EPS = 1e-5
NEG = -1.0e30

ENGINES = ("sync", "gpsimd", "tensor", "vector", "scalar")


def tmax(*tokens):
    best = {}
    for t in tokens:
        if t is None:
            continue
        if isinstance(t, (list, tuple)) and t and isinstance(t[0], (list, tuple)):
            for u in t:
                if u is None:
                    continue
                s, v = u
                if best.get(s, -1) < v:
                    best[s] = v
            continue
        s, v = t
        if best.get(s, -1) < v:
            best[s] = v
    return tuple(best.items())


class Plan:
    def __init__(self):
        self.steps = []
        self.ctr = {}

    def emit(self, engine, fn, waits=(), inc=None, amt=1):
        self.steps.append((engine, fn, tmax(*waits) if waits else (), inc, amt))
        if inc is None:
            return None
        self.ctr[inc] = self.ctr.get(inc, 0) + amt
        return (inc, self.ctr[inc])

    def dma(self, engine, out, in_, waits=(), sem=None):
        assert sem is not None, "every DMA needs a stream semaphore"
        return self.emit(engine, lambda e, o=out, i=in_: e.dma_start(o, i),
                         waits=waits, inc=sem, amt=16)


def build_program(debug=False):
    nc = bass.Bass("TRN2", target_bir_lowering=False, debug=False)

    def din(name, shape, dt=F32):
        return nc.dram_tensor(name, list(shape), dt, kind="ExternalInput").ap()

    xt_in = din("xt_in", (NTILE, 128, NCH * 512))
    xres_in = din("xres_in", (NTOK, D))
    wq_in = din("wq_in", (128, NCH * 256))
    wk_in = din("wk_in", (128, NCH * 256))
    wv_in = din("wv_in", (128, NCH * 256))
    wo_in = din("wo_in", (128, NCH * D))
    w1_in = din("w1_in", (32, 128, NCH * 128))
    w2_in = din("w2_in", (32, 128, NCH * 128))
    h1_in = din("h1_in", (32, 128, 32 * 128))
    h2_in = din("h2_in", (32, 128, 32 * 128))
    w3_in = din("w3_in", (32, 128, D))
    w1s_in = din("w1s_in", (128, 32))
    w2s_in = din("w2s_in", (128, 32))
    h1s_in = din("h1s_in", (128, 32))
    h2s_in = din("h2s_in", (128, 32))
    w3sb_in = din("w3sb_in", (128, D))
    cos_in = din("cos_in", (128, T))
    sin_in = din("sin_in", (128, T))
    rmat_in = din("rmat_in", (128, 128), BF16)
    ident_in = din("ident_in", (128, 128), BF16)
    indic_in = din("indic_in", (128, 33), BF16)
    ones1_in = din("ones1_in", (1, 128))
    sel_lo_in = din("sel_lo_in", (1, 128))
    sel_hi_in = din("sel_hi_in", (1, 128))
    masks_in = din("masks_in", (128, 4 * 1024))
    off_in = din("off_in", (1, 8), I32)

    y_out = nc.dram_tensor("y_out", [NTOK, D], F32, kind="ExternalOutput").ap()
    if debug:
        qT_dbg = nc.dram_tensor("qT_dbg", [128, 2 * T], BF16, kind="ExternalOutput").ap()
        kT_dbg = nc.dram_tensor("kT_dbg", [128, 2 * T], BF16, kind="ExternalOutput").ap()
        v_dbg = nc.dram_tensor("v_dbg", [128, 64 * 130], BF16, kind="ExternalOutput").ap()
        ag_dbg = nc.dram_tensor("ag_dbg", [1024, T], BF16, kind="ExternalOutput").ap()
        x1_dbg = nc.dram_tensor("x1_dbg", [NTOK, D], F32, kind="ExternalOutput").ap()
        u_dbg = nc.dram_tensor("u_dbg", [32, 128, NTOK], BF16, kind="ExternalOutput").ap()
        bias_dbg = nc.dram_tensor("bias_dbg", [128, 64], F32, kind="ExternalOutput").ap()

    ot_dram = [nc.dram_tensor(f"ot_scratch{i}", [128, T], BF16, kind="Internal").ap() for i in range(2)]
    ag_dram = [nc.dram_tensor(f"ag_scratch{i}", [512, T], BF16, kind="Internal").ap() for i in range(2)]
    x1_dram = nc.dram_tensor("x1_scratch", [NTOK, D], F32, kind="Internal").ap()
    u_dram = nc.dram_tensor("u_scratch", [32, 128, NTOK], BF16, kind="Internal").ap()
    w3b_dram = nc.dram_tensor("w3b_scratch", [32, 128, D], BF16, kind="Internal").ap()

    P = Plan()
    ctx = ExitStack()
    sb = lambda n, s, dt: ctx.enter_context(nc.sbuf_tensor(n, list(s), dt))
    attn_ctx = ExitStack()
    sba = lambda n, s, dt: attn_ctx.enter_context(nc.sbuf_tensor(n, list(s), dt))

    wo = sb("wo", (128, NCH * D), BF16)
    ident = sb("ident", (128, 128), BF16)
    w3sb = sb("w3sb", (128, D), F32)
    scal = {n: sb(n, (128, 32), F32) for n in ("w1s", "w2s", "h1s", "h2s")}
    off_sb = sb("off_sb", (1, 8), I32)
    qT = sba("qT", (128, 2 * T), BF16)
    kT = sba("kT", (128, 2 * T), BF16)
    vsb = [sba(f"v{cc}", (128, 64 * 130), BF16) for cc in range(2)]
    wq = sba("wq", (128, NCH * 256), BF16)
    wk = sba("wk", (128, NCH * 256), BF16)
    wv = sba("wv", (128, NCH * 256), BF16)
    rmat = sba("rmat", (128, 128), BF16)
    indic = sba("indic", (128, 33), BF16)
    ones1 = sba("ones1", (1, 128), F32)
    sel_lo = sba("sel_lo", (1, 128), F32)
    sel_hi = sba("sel_hi", (1, 128), F32)
    masks = sba("masks", (128, 4 * 1024), F32)
    qstat = sba("qstat", (1, 64), F32)
    kstat = sba("kstat", (1, 64), F32)
    kmaxr = sba("kmaxr", (1, 4), F32)
    crow = sba("crow", (1, 64), F32)
    crow2 = sba("crow2", (1, 32), F32)
    bias_all = sba("bias_all", (128, 32), F32)

    rgo = ctx.enter_context(nc.gpsimd.register("rgo"))

    for dst, src in ((wq, wq_in), (wk, wk_in), (wv, wv_in), (wo, wo_in)):
        t_cg = P.dma("gpsimd", dst[:, :], src, sem="cg")
    for dst, src in ((rmat, rmat_in), (ident, ident_in), (indic, indic_in),
                     (ones1, ones1_in), (sel_lo, sel_lo_in), (sel_hi, sel_hi_in),
                     (masks, masks_in), (w3sb, w3sb_in), (off_sb, off_in)):
        t_cs = P.dma("sync", dst[:, :], src, sem="cs")
    for n, src in (("w1s", w1s_in), ("w2s", w2s_in), ("h1s", h1s_in), ("h2s", h2s_in)):
        t_cs = P.dma("sync", scal[n][:, :], src, sem="cs")
    t_wq = t_wk = t_wv = t_wo = t_cg
    t_rm = t_id = t_ic = t_o1 = t_sl = t_sh = t_mk = t_w3b = t_off = t_cs
    t_scal = {n: t_cs for n in ("w1s", "w2s", "h1s", "h2s")}
    v3 = [v.rearrange("p (b c) -> p b c", c=130) for v in vsb]
    t_vones = []
    for cc in range(2):
        t_vones.append(P.emit("vector", lambda e, a=v3[cc]: e.memset(a[:, :, 64:65], 1.0), inc="DVE"))
        t_vones.append(P.emit("vector", lambda e, a=v3[cc]: e.memset(a[:, :, 129:130], 1.0), inc="DVE"))

    p1ctx = ExitStack()
    sb1 = lambda n, s, dt: p1ctx.enter_context(nc.sbuf_tensor(n, list(s), dt))
    ps1 = lambda n, s, dt=F32: p1ctx.enter_context(nc.psum_tensor(n, list(s), dt))
    xts = [sb1(f"xt{i}", (128, NCH * 512), BF16) for i in range(2)]
    coss = [sb1(f"cos{i}", (128, 512), F32) for i in range(2)]
    sins = [sb1(f"sin{i}", (128, 512), F32) for i in range(2)]
    k16 = [sb1(f"k16_{i}", (128, 512), BF16) for i in range(2)]
    ropeA = [sb1(f"ropeA{i}", (128, 512), F32) for i in range(2)]
    ropeB = [sb1(f"ropeB{i}", (128, 512), F32) for i in range(2)]
    sqst = [sb1(f"sqst{i}", (128, 512), BF16) for i in range(2)]
    pk = [ps1(f"pk{i}", (128, 512)) for i in range(2)]
    prot = [ps1(f"prot{i}", (128, 512)) for i in range(2)]
    pv = [ps1(f"pv{i}", (128, 256)) for i in range(2)]
    pstat = [ps1(f"pstat{i}", (33, 512)) for i in range(2)]

    t_xt = [None, None]
    xt_users = [[], []]
    t_cos = [None, None]
    t_sin = [None, None]
    tab_users = [[], []]
    pk_free = [[], []]
    k16_free = [None, None]
    ropeA_free = [None, None]
    ropeB_free = [None, None]
    sqst_free = [None, None]
    prot_free = [None, None]
    pv_free = [[], []]
    pstat_free = [[], []]
    t_kT = [None] * NTILE
    t_qT = [None] * NTILE
    t_V = [None] * NTILE
    stat_toks = []

    pcount = 0
    for i in range(NTILE):
        t0 = i * 512
        sl = i % 2
        t_xt[sl] = P.dma("gpsimd", xts[sl][:, :], xt_in[i, :, :], waits=xt_users[sl],
                         sem=f"xt{sl}")
        xt_users[sl] = []
        t_cos[sl] = P.dma("sync", coss[sl][:, :], cos_in[:, t0:t0 + 512], waits=tab_users[sl],
                          sem=f"cos{sl}")
        t_sin[sl] = P.dma("sync", sins[sl][:, :], sin_in[:, t0:t0 + 512], waits=tab_users[sl],
                          sem=f"sin{sl}")
        tab_users[sl] = []
        for cc in range(2):
            for kind in ("k", "q"):
                wsb, t_w = (wk, t_wk) if kind == "k" else (wq, t_wq)
                dest = kT if kind == "k" else qT
                pp = pcount % 2
                pcount += 1

                def mk_proj(e, wsb=wsb, cc=cc, sl=sl, pp=pp):
                    last = None
                    for kc in range(NCH):
                        last = e.matmul(pk[pp][:, :],
                                        wsb[:, kc * 256 + cc * 128:kc * 256 + cc * 128 + 128],
                                        xts[sl][:, kc * 512:kc * 512 + 512],
                                        start=(kc == 0), stop=(kc == NCH - 1))
                    return last
                tok_mm = P.emit("tensor", mk_proj,
                                waits=[t_xt[sl], t_w] + pk_free[pp], inc="PE")
                xt_users[sl].append(tok_mm)
                tok_cp = P.emit("vector",
                                lambda e, pp=pp: e.tensor_copy(k16[pp][:, :], pk[pp][:, :]),
                                waits=[tok_mm, k16_free[pp]], inc="DVE")
                tok_rot = P.emit("tensor",
                                 lambda e, pp=pp: e.matmul(prot[pp][:, :], rmat[:, :], k16[pp][:, :],
                                                           start=True, stop=True),
                                 waits=[tok_cp, t_rm, prot_free[pp]], inc="PE")
                k16_free[pp] = tok_rot
                tok_a = P.emit("vector",
                               lambda e, pp=pp, sl=sl: e.tensor_mul(ropeA[pp][:, :], pk[pp][:, :], coss[sl][:, :]),
                               waits=[tok_mm, t_cos[sl], ropeA_free[pp]], inc="DVE")
                tok_b = P.emit("vector",
                               lambda e, pp=pp, sl=sl: e.tensor_mul(ropeB[pp][:, :], prot[pp][:, :], sins[sl][:, :]),
                               waits=[tok_rot, t_sin[sl], ropeB_free[pp]], inc="DVE")
                prot_free[pp] = tok_b
                dslice = dest[:, cc * T + t0:cc * T + t0 + 512]
                tok_out = P.emit("vector",
                                 lambda e, d=dslice, pp=pp: e.tensor_add(d, ropeA[pp][:, :], ropeB[pp][:, :]),
                                 waits=[tok_a, tok_b], inc="DVE")
                pk_free[pp] = [tok_cp, tok_a]
                ropeA_free[pp] = tok_out
                ropeB_free[pp] = tok_out
                tab_users[sl] += [tok_a, tok_b]
                tok_sq = P.emit("scalar",
                                lambda e, d=dslice, pp=pp: e.activation(sqst[pp][:, :], d, AF.Square),
                                waits=[tok_out, sqst_free[pp]], inc="ACT")
                tok_sm = P.emit("tensor",
                                lambda e, pp=pp: e.matmul(pstat[pp][:, :], indic[:, :], sqst[pp][:, :],
                                                          start=True, stop=True),
                                waits=[tok_sq, t_ic] + pstat_free[pp], inc="PE")
                sqst_free[pp] = tok_sm
                sdst = kstat if kind == "k" else qstat
                r_toks = []
                for ab in range(2):
                    col = (cc * 2 + ab) * 16 + i
                    row = ab * 32
                    r_toks.append(P.emit("vector",
                                         lambda e, sd=sdst, c=col, r=row, pp=pp:
                                         e.reduce_max(sd[0:1, c:c + 1], pstat[pp][r:r + 1, :], axis=AX),
                                         waits=[tok_sm], inc="DVE"))
                pstat_free[pp] = r_toks
                stat_toks += r_toks
                if kind == "k":
                    t_kT[i] = tok_out if t_kT[i] is None else ("DVE", max(t_kT[i][1], tok_out[1]))
                else:
                    t_qT[i] = tok_out if t_qT[i] is None else ("DVE", max(t_qT[i][1], tok_out[1]))
        for st in range(4):
            pp = st % 2
            blk = i * 4 + st

            def mk_vproj(e, sl=sl, st=st, pp=pp):
                last = None
                for kc in range(NCH):
                    last = e.matmul(pv[pp][:, :],
                                    xts[sl][:, kc * 512 + st * 128:kc * 512 + st * 128 + 128],
                                    wv[:, kc * 256:kc * 256 + 256],
                                    start=(kc == 0), stop=(kc == NCH - 1))
                return last
            tok_vm = P.emit("tensor", mk_vproj,
                            waits=[t_xt[sl], t_wv] + pv_free[pp], inc="PE")
            xt_users[sl].append(tok_vm)
            cps = []
            for cc in range(2):
                for ab in range(2):
                    s0c = cc * 128 + ab * 64
                    d0c = blk * 130 + ab * 65
                    cps.append(P.emit("vector",
                                      lambda e, cc=cc, s0c=s0c, d0c=d0c, pp=pp:
                                      e.tensor_copy(vsb[cc][:, d0c:d0c + 64], pv[pp][:, s0c:s0c + 64]),
                                      waits=[tok_vm], inc="DVE"))
            pv_free[pp] = cps
            t_V[i] = cps[-1] if t_V[i] is None else ("DVE", max(t_V[i][1], cps[-1][1]))

    t_allstat = stat_toks
    tok_ks = P.emit("scalar", lambda e: e.activation(kstat[:, :], kstat[:, :], AF.Sqrt),
                    waits=t_allstat, inc="ACT")
    tok_qs = P.emit("scalar", lambda e: e.activation(qstat[:, :], qstat[:, :], AF.Sqrt),
                    waits=t_allstat, inc="ACT")
    tok_km = []
    for hl in range(4):
        tok_km.append(P.emit("vector",
                             lambda e, hl=hl: e.reduce_max(kmaxr[0:1, hl:hl + 1],
                                                           kstat[0:1, hl * 16:(hl + 1) * 16], axis=AX),
                             waits=[tok_ks], inc="DVE"))
    tok_crs = []
    for hl in range(4):
        tok_crs.append(P.emit("vector",
                              lambda e, hl=hl: e.tensor_scalar_mul(crow[0:1, hl * 16:(hl + 1) * 16],
                                                                   qstat[0:1, hl * 16:(hl + 1) * 16],
                                                                   kmaxr[0:1, hl:hl + 1]),
                              waits=[tok_qs] + tok_km, inc="DVE"))
    tok_cr = tok_crs[-1]
    tok_cr2 = P.emit("vector",
                     lambda e: e.tensor_scalar(crow[:, :], crow[:, :], -0.125, -1.0, ALU.mult, ALU.add),
                     waits=tok_crs, inc="DVE")
    cr_ev = bass.AP(crow, 0, [[64, 1], [32, 2], [1, 16]])
    cr_od = bass.AP(crow, 16, [[64, 1], [32, 2], [1, 16]])
    cr2_v = crow2[0:1, :].rearrange("a (c t) -> a c t", c=2)
    tok_pm = P.emit("vector", lambda e: e.tensor_tensor(cr2_v, cr_ev, cr_od, ALU.min),
                    waits=[tok_cr2], inc="DVE")
    tok_bmm = P.emit("tensor", lambda e: e.matmul(pk[0][:, 0:32], ones1[:, :], crow2[:, :],
                                                  start=True, stop=True),
                     waits=[tok_pm, t_o1] + pk_free[0], inc="PE")
    t_bias = P.emit("vector", lambda e: e.tensor_copy(bias_all[:, :], pk[0][:, 0:32]),
                    waits=[tok_bmm], inc="DVE")
    t_p1 = [t_bias] + t_kT + t_qT + t_V + t_vones
    w3cast_toks = [P.dma("gpsimd", w3b_dram[fc, :, :], w3_in[fc, :, :], sem="w3c")
                   for fc in range(32)]
    if debug:
        P.dma("sync", qT_dbg, qT[:, :], waits=t_p1, sem="dbg")
        P.dma("sync", kT_dbg, kT[:, :], waits=t_p1, sem="dbg")
        P.dma("sync", v_dbg, vsb[0][:, :], waits=t_p1, sem="dbg")
        P.dma("sync", bias_dbg, bias_all[:, :], waits=t_p1, sem="dbg")
    p1ctx.close()

    p2ctx = ExitStack()
    sb2 = lambda n, s, dt: p2ctx.enter_context(nc.sbuf_tensor(n, list(s), dt))
    ps2 = lambda n, s, dt=F32: p2ctx.enter_context(nc.psum_tensor(n, list(s), dt))
    pbuf2 = sb2("pbuf2", (128, 4 * 1024), BF16)
    otsb = [sb2(f"otsb{i}", (128, 512), BF16) for i in range(2)]
    rrow = [sb2(f"rrow{h}", (1, 512), F32) for h in range(2)]
    lrow = [sb2(f"lrow{h}", (1, 512), F32) for h in range(2)]
    rbc_sb = sb2("rbc_sb", (128, 512), F32)
    pscore2 = [ps2(f"psc2_{p}", (128, 1024)) for p in range(3)]
    pav = [ps2(f"pav{h}", (65, 512)) for h in range(2)]

    exp_tok = {}
    av_tok = {}
    evac_tok = [None, None]
    pav_free = []
    last_rc = [None]
    rbc_free = [[]]
    ot_dma = []
    cc_done = [None, None]

    for cc in range(2):
        for ttb in range(NTILE):
            nchunk = 4 * (ttb + 1)
            etoks_arr = [None] * nchunk

            def emit_sc_exp(ci, ttb=ttb, cc=cc, nchunk=nchunk, etoks_arr=etoks_arr):
                s0 = ci * 128
                par = ci % 3
                par4 = ci % 4
                diag = ci >= 4 * ttb
                sc_toks = []
                for ab in range(2):
                    lhs = kT[ab * 64:(ab + 1) * 64, cc * T + s0:cc * T + s0 + 128]
                    rhs = qT[ab * 64:(ab + 1) * 64, cc * T + ttb * 512:cc * T + (ttb + 1) * 512]
                    sc_toks.append(P.emit("tensor",
                                          lambda e, ab=ab, par=par, l=lhs, r=rhs:
                                          e.matmul(pscore2[par][:, ab * 512:ab * 512 + 512], l, r,
                                                   start=True, stop=True, tile_position=(ab * 64, 0)),
                                          waits=[t_kT[ci // 4], t_qT[ttb], exp_tok.get(par),
                                                 last_rc[0] if (ci == 0 and par == 0) else None] + t_p1,
                                          inc="PE"))
                mtk = None
                if diag:
                    r = (ci - 4 * ttb) * 128
                    mtk = P.emit("vector",
                                 lambda e, par=par, r=r:
                                 e.tensor_add(pscore2[par][:, :], pscore2[par][:, :],
                                              masks[:, (r // 128) * 1024:(r // 128) * 1024 + 1024]),
                                 waits=sc_toks + [t_mk], inc="DVE")
                bc = cc * 16 + ttb
                etok = P.emit("scalar",
                              lambda e, par=par, par4=par4, bc=bc:
                              e.activation(pbuf2[:, par4 * 1024:par4 * 1024 + 1024],
                                           pscore2[par][:, :], AF.Exp,
                                           bias=bias_all[:, bc:bc + 1], scale=0.125),
                              waits=([mtk] if mtk else sc_toks) + [t_bias,
                                    av_tok.get((0, par4)), av_tok.get((1, par4))],
                              inc="ACT")
                exp_tok[par] = etok
                etoks_arr[ci] = etok

            def emit_av(ci, ttb=ttb, cc=cc, nchunk=nchunk, etoks_arr=etoks_arr):
                par4 = ci % 4
                for ab in range(2):
                    lhsv = vsb[cc][:, ci * 130 + ab * 65:ci * 130 + ab * 65 + 65]
                    first, last = ci == 0, ci == nchunk - 1
                    av_tok[(ab, par4)] = P.emit("tensor",
                                                lambda e, ab=ab, par4=par4, lv=lhsv, f=first, lst=last:
                                                e.matmul(pav[ab][:, :], lv,
                                                         pbuf2[:, par4 * 1024 + ab * 512:par4 * 1024 + ab * 512 + 512],
                                                         start=f, stop=lst),
                                                waits=[etoks_arr[ci], t_V[ci // 4]] +
                                                      (pav_free if first else []),
                                                inc="PE")

            emit_sc_exp(0)
            if nchunk > 1:
                emit_sc_exp(1)
            if nchunk > 2:
                emit_sc_exp(2)
            for ci in range(nchunk):
                if ci + 3 < nchunk:
                    emit_sc_exp(ci + 3)
                emit_av(ci)

            rtoks = []
            for ab in range(2):
                t_ln = P.emit("scalar",
                              lambda e, ab=ab: e.activation(lrow[ab][:, :], pav[ab][64:65, :], AF.Ln),
                              waits=[av_tok.get((ab, p4)) for p4 in range(4)], inc="ACT")
                rtoks.append(P.emit("scalar",
                                    lambda e, ab=ab: e.activation(rrow[ab][:, :], lrow[ab][:, :], AF.Exp,
                                                                  scale=-1.0),
                                    waits=[t_ln], inc="ACT"))

            def mk_rbc(e):
                e.matmul(pscore2[0][:, 0:512], sel_lo[:, :], rrow[0][:, :], start=True, stop=False)
                return e.matmul(pscore2[0][:, 0:512], sel_hi[:, :], rrow[1][:, :], start=False, stop=True)
            tok_rb = P.emit("tensor", mk_rbc,
                            waits=[rtoks[0], rtoks[1], t_sl, t_sh, exp_tok.get(0)], inc="PE")
            tok_rc = P.emit("vector", lambda e: e.tensor_copy(rbc_sb[:, :], pscore2[0][:, 0:512]),
                            waits=[tok_rb] + rbc_free[0], inc="DVE")
            last_rc[0] = tok_rc
            osl = ttb % 2
            ev = []
            ev.append(P.emit("vector",
                             lambda e, osl=osl: e.tensor_mul(otsb[osl][0:64, :], pav[0][0:64, :], rbc_sb[0:64, :]),
                             waits=[tok_rc, evac_tok[osl]], inc="DVE"))
            ev.append(P.emit("vector",
                             lambda e, osl=osl: e.tensor_mul(otsb[osl][64:128, :], pav[1][0:64, :], rbc_sb[64:128, :]),
                             waits=[tok_rc, evac_tok[osl]], inc="DVE"))
            pav_free = ev + rtoks
            rbc_free[0] = ev
            evac_tok[osl] = P.dma("sync", ot_dram[cc][:, ttb * 512:(ttb + 1) * 512],
                                  otsb[osl][:, :], waits=ev, sem="ots")
            ot_dma.append(evac_tok[osl])
        cc_done[cc] = list(ot_dma)
        ot_dma = []

    for cc in range(2):
        t_cc = P.emit("gpsimd",
                      lambda e, cc=cc: e.collective_compute("AllGather", ALU.bypass,
                                                            replica_groups=[[0, 1, 2, 3], [4, 5, 6, 7]],
                                                            ins=[ot_dram[cc][:, :].opt()],
                                                            outs=[ag_dram[cc][:, :].opt()]),
                      waits=cc_done[cc], inc="CC")
    if debug:
        P.dma("sync", ag_dbg, ag_dram[:, :], waits=[t_cc], sem="dbg")
    p2ctx.close()
    attn_ctx.close()

    x1T_ctx = ExitStack()
    x1T = x1T_ctx.enter_context(nc.sbuf_tensor("x1T", [128, NCH * NTOK], BF16))
    pwctx = ExitStack()
    sbw = lambda n, s, dt: pwctx.enter_context(nc.sbuf_tensor(n, list(s), dt))
    psw = lambda n, s, dt=F32: pwctx.enter_context(nc.psum_tensor(n, list(s), dt))
    go = sbw("go", (128, NCH * NTOK), BF16)
    xres = [sbw(f"xres{i}", (128, D), F32) for i in range(2)]
    x1st = [sbw(f"x1st{i}", (128, D), F32) for i in range(2)]
    x116 = [sbw(f"x116{i}", (128, D), BF16) for i in range(2)]
    st12 = [sbw(f"st12_{i}", (128, 12), F32) for i in range(2)]
    st2 = [sbw(f"st2_{i}", (128, 2), F32) for i in range(2)]
    stdv = [sbw(f"stdv{i}", (128, 1), F32) for i in range(2)]
    rstd = [sbw(f"rstd{i}", (128, 1), F32) for i in range(2)]
    po = [[psw(f"po{h}_{p}", (128, 512)) for p in range(2)] for h in range(2)]
    ptr = [psw(f"ptr{p}", (128, 1024), BF16) for p in range(2)]

    go_toks = []
    for dc in range(NCH):
        def mk_go(e, dc=dc):
            e.reg_load(rgo, off_sb[0:1, dc:dc + 1])
            return e.dma_start(go[:, dc * NTOK:(dc + 1) * NTOK],
                               bass.AP(ag_dram[dc % 2].tensor, rgo, [[T, 128], [1, NTOK]]))
        go_toks.append(P.emit("gpsimd", mk_go, waits=[("CC", 2), t_off], inc="go", amt=16))

    t_xres = [None, None]
    xres_users = [[], []]
    po_free = [[[], []], [[], []]]
    x1st_free = [None, None]
    x116_free = [None, None]
    ptr_free = [[], []]
    x1T_toks = []
    x1d_toks = []
    for tt in range(NTILE):
        sl = tt % 2
        t_xres[sl] = P.dma("sync", xres[sl][:, :], xres_in[tt * 128:(tt + 1) * 128, :],
                           waits=xres_users[sl], sem=f"xr{sl}")
        xres_users[sl] = []
        mm_toks = []
        for eh in range(2):
            def mk_wo(e, tt=tt, eh=eh, sl=sl):
                last = None
                for dc in range(NCH):
                    last = e.matmul(po[eh][sl][:, :],
                                    go[:, dc * NTOK + tt * 128:dc * NTOK + tt * 128 + 128],
                                    wo[:, dc * D + eh * 512:dc * D + eh * 512 + 512],
                                    start=(dc == 0), stop=(dc == NCH - 1))
                return last
            mm_toks.append(P.emit("tensor", mk_wo,
                                  waits=go_toks + [t_wo] + po_free[eh][sl], inc="PE"))
        add_toks = []
        for eh in range(2):
            add_toks.append(P.emit("vector",
                                   lambda e, eh=eh, sl=sl:
                                   e.tensor_add(x1st[sl][:, eh * 512:eh * 512 + 512],
                                                po[eh][sl][:, :], xres[sl][:, eh * 512:eh * 512 + 512]),
                                   waits=[mm_toks[eh], t_xres[sl], x1st_free[sl]], inc="DVE"))
        xres_users[sl] += add_toks
        for eh in range(2):
            po_free[eh][sl] = [add_toks[eh]]
        bn_toks = []
        for eh in range(2):
            bn_toks.append(P.emit("vector",
                                  lambda e, eh=eh, sl=sl:
                                  e.bn_stats(st12[sl][:, eh * 6:eh * 6 + 6], x1st[sl][:, eh * 512:eh * 512 + 512]),
                                  waits=[add_toks[eh]], inc="DVE"))
        tok_ag = P.emit("vector", lambda e, sl=sl: e.bn_aggr(st2[sl][:, :], st12[sl][:, :]),
                        waits=bn_toks, inc="DVE")
        tok_ve = P.emit("vector", lambda e, sl=sl: e.tensor_scalar_add(stdv[sl][:, :], st2[sl][:, 1:2], EPS),
                        waits=[tok_ag], inc="DVE")
        tok_sd = P.emit("scalar", lambda e, sl=sl: e.activation(stdv[sl][:, :], stdv[sl][:, :], AF.Sqrt),
                        waits=[tok_ve], inc="ACT")
        tok_rs = P.emit("vector", lambda e, sl=sl: e.reciprocal(rstd[sl][:, :], stdv[sl][:, :]),
                        waits=[tok_sd], inc="DVE")
        tok_x1 = P.emit("vector",
                        lambda e, sl=sl: e.tensor_scalar(x1st[sl][:, :], x1st[sl][:, :],
                                                         st2[sl][:, 0:1], rstd[sl][:, :],
                                                         ALU.subtract, ALU.mult),
                        waits=[tok_rs], inc="DVE")
        tok_xd = P.dma("sync", x1_dram[tt * 128:(tt + 1) * 128, :], x1st[sl][:, :], waits=[tok_x1],
                       sem="x1s")
        x1d_toks.append(tok_xd)
        tok_c16 = P.emit("vector", lambda e, sl=sl: e.tensor_copy(x116[sl][:, :], x1st[sl][:, :]),
                         waits=[tok_x1, x116_free[sl]], inc="DVE")
        x1st_free[sl] = [tok_xd, tok_c16]

        def mk_tr(e, sl=sl):
            last = None
            for ec in range(NCH):
                last = e.transpose(ptr[sl][:, ec * 128:ec * 128 + 128],
                                   x116[sl][:, ec * 128:ec * 128 + 128], ident[:, :])
            return last
        tok_tr = P.emit("tensor", mk_tr, waits=[tok_c16, t_id] + ptr_free[sl], inc="PE")
        x116_free[sl] = tok_tr
        evs = []
        for ec in range(NCH):
            evs.append(P.emit("vector",
                              lambda e, ec=ec, tt=tt, sl=sl:
                              e.tensor_copy(x1T[:, ec * NTOK + tt * 128:ec * NTOK + tt * 128 + 128],
                                            ptr[sl][:, ec * 128:ec * 128 + 128]),
                              waits=[tok_tr], inc="DVE"))
        ptr_free[sl] = evs
        x1T_toks += evs
    t_pw_pe = P.ctr["PE"]
    pwctx.close()

    f1ctx = ExitStack()
    sbf = lambda n, s, dt: f1ctx.enter_context(nc.sbuf_tensor(n, list(s), dt))
    psf = lambda n, s, dt=F32: f1ctx.enter_context(nc.psum_tensor(n, list(s), dt))
    w1c = [sbf(f"w1c{i}", (128, NCH * 128), BF16) for i in range(2)]
    w2c = [sbf(f"w2c{i}", (128, NCH * 128), BF16) for i in range(2)]
    asbuf = [sbf(f"asb{i}", (128, 512), BF16) for i in range(2)]
    u16 = [sbf(f"u16_{i}", (128, 512), BF16) for i in range(2)]
    pu1 = [psf(f"pu1_{i}", (128, 512)) for i in range(2)]
    pu2 = [psf(f"pu2_{i}", (128, 512)) for i in range(2)]

    t_pw = [("PE", t_pw_pe)] + x1d_toks[-2:] + x1T_toks[-2:]
    w1_tok = [None, None]
    w2_tok = [None, None]
    w_users = [[], []]
    pu1_free = [[], []]
    pu2_free = [[], []]
    a_free = [None, None]
    u16_free = [None, None]
    u_dma = []
    for mc in range(32):
        sl = mc % 2
        w1_tok[sl] = P.dma("gpsimd", w1c[sl][:, :], w1_in[mc, :, :], waits=w_users[sl] + t_pw,
                           sem=f"w1c{sl}")
        w2_tok[sl] = P.dma("gpsimd", w2c[sl][:, :], w2_in[mc, :, :], waits=w_users[sl] + t_pw,
                           sem=f"w2c{sl}")
        w_users[sl] = []
        for tq in range(4):
            pp = tq % 2

            def mk_u(e, sl=sl, tq=tq, pp=pp, wsel=0):
                wc = w1c[sl] if wsel == 0 else w2c[sl]
                pu = pu1[pp] if wsel == 0 else pu2[pp]
                last = None
                for kc in range(NCH):
                    last = e.matmul(pu[:, :], wc[:, kc * 128:kc * 128 + 128],
                                    x1T[:, kc * NTOK + tq * 512:kc * NTOK + tq * 512 + 512],
                                    start=(kc == 0), stop=(kc == NCH - 1))
                return last
            tok_m1 = P.emit("tensor", lambda e, f=mk_u: f(e, wsel=0),
                            waits=[w1_tok[sl]] + x1T_toks + pu1_free[pp], inc="PE")
            tok_m2 = P.emit("tensor", lambda e, f=mk_u: f(e, wsel=1),
                            waits=[w2_tok[sl]] + pu2_free[pp], inc="PE")
            w_users[sl] += [tok_m1, tok_m2]
            tok_si = P.emit("scalar",
                            lambda e, pp=pp, mc=mc: e.activation(asbuf[pp][:, :], pu1[pp][:, :], AF.Silu,
                                                                 scale=scal["w1s"][:, mc:mc + 1]),
                            waits=[tok_m1, t_scal["w1s"], a_free[pp]], inc="ACT")
            pu1_free[pp] = [tok_si]
            tok_u = P.emit("vector",
                           lambda e, pp=pp, mc=mc: e.scalar_tensor_tensor(u16[pp][:, :], pu2[pp][:, :],
                                                                          scal["w2s"][:, mc:mc + 1],
                                                                          asbuf[pp][:, :],
                                                                          ALU.mult, ALU.mult),
                           waits=[tok_m2, tok_si, t_scal["w2s"], u16_free[pp]], inc="DVE")
            pu2_free[pp] = [tok_u]
            a_free[pp] = tok_u
            tok_ud = P.dma("sync", u_dram[mc, :, tq * 512:(tq + 1) * 512], u16[pp][:, :], waits=[tok_u],
                           sem="u16")
            u16_free[pp] = tok_ud
            u_dma.append(tok_ud)
    t_f1_pe = P.ctr["PE"]
    if debug:
        P.dma("sync", x1_dbg, x1_dram[:, :], waits=x1d_toks, sem="dbg")
        P.dma("sync", u_dbg, u_dram[:, :, :], waits=u_dma, sem="dbg")
    f1ctx.close()
    x1T_ctx.close()

    f2ctx = ExitStack()
    sbg = lambda n, s, dt: f2ctx.enter_context(nc.sbuf_tensor(n, list(s), dt))
    psg = lambda n, s, dt=F32: f2ctx.enter_context(nc.psum_tensor(n, list(s), dt))
    u2sb = sbg("u2sb", (128, 32 * 1024), BF16)
    big2 = sbg("big2", (128, 32 * 1024), BF16)
    hc = [sbg(f"hc{i}", (128, 32 * 128), BF16) for i in range(2)]
    yst = [sbg(f"yst{i}", (128, D), F32) for i in range(2)]
    x1r = [sbg(f"x1r{i}", (128, D), F32) for i in range(2)]
    outs = [sbg(f"outs{i}", (128, D), F32) for i in range(2)]
    st12b = [sbg(f"st12b{i}", (128, 12), F32) for i in range(2)]
    st2b = [sbg(f"st2b{i}", (128, 2), F32) for i in range(2)]
    stdb = [sbg(f"stdb{i}", (128, 1), F32) for i in range(2)]
    rstb = [sbg(f"rstb{i}", (128, 1), F32) for i in range(2)]
    ph = [psg(f"ph{i}", (128, 512)) for i in range(2)]
    py = [[psg(f"py{h}_{p}", (128, 512)) for p in range(2)] for h in range(2)]

    t_f1 = [("PE", t_f1_pe)] + u_dma[-2:]
    y_dma = []
    u2_users_prev = []
    for th in range(2):
        u_toks = []
        for fc in range(32):
            u_toks.append(P.dma("sync", u2sb[:, fc * 1024:(fc + 1) * 1024],
                                u_dram[fc, :, th * 1024:(th + 1) * 1024],
                                waits=u_dma + t_f1 + u2_users_prev, sem="ut"))
        hc_tok = [None, None]
        hc_users = [[], []]
        ph_free = [[], []]
        h1_acts = []
        for mc in range(32):
            sl = mc % 2
            hc_tok[sl] = P.dma("gpsimd", hc[sl][:, :], h1_in[mc, :, :], waits=hc_users[sl] + t_f1,
                               sem=f"hc{sl}")
            hc_users[sl] = []
            for ts in range(2):
                pp = ts % 2

                def mk_h(e, sl=sl, ts=ts, pp=pp, src=u2sb):
                    last = None
                    for kc in range(32):
                        last = e.matmul(ph[pp][:, :], hc[sl][:, kc * 128:kc * 128 + 128],
                                        src[:, kc * 1024 + ts * 512:kc * 1024 + ts * 512 + 512],
                                        start=(kc == 0), stop=(kc == 31))
                    return last
                tok_hm = P.emit("tensor", mk_h,
                                waits=[hc_tok[sl]] + u_toks + ph_free[pp], inc="PE")
                hc_users[sl].append(tok_hm)
                tok_ha = P.emit("scalar",
                                lambda e, mc=mc, ts=ts, pp=pp:
                                e.activation(big2[:, mc * 1024 + ts * 512:mc * 1024 + ts * 512 + 512],
                                             ph[pp][:, :], AF.Gelu, scale=scal["h1s"][:, mc:mc + 1]),
                                waits=[tok_hm, t_scal["h1s"]], inc="ACT")
                ph_free[pp] = [tok_ha]
                h1_acts.append(tok_ha)
        h2_acts = []
        h1_mm_done = ("PE", P.ctr["PE"])
        for mc in range(32):
            sl = mc % 2
            hc_tok[sl] = P.dma("gpsimd", hc[sl][:, :], h2_in[mc, :, :], waits=hc_users[sl],
                               sem=f"hc{sl}")
            hc_users[sl] = []
            for ts in range(2):
                pp = ts % 2

                def mk_h2(e, sl=sl, ts=ts, pp=pp):
                    last = None
                    for kc in range(32):
                        last = e.matmul(ph[pp][:, :], hc[sl][:, kc * 128:kc * 128 + 128],
                                        big2[:, kc * 1024 + ts * 512:kc * 1024 + ts * 512 + 512],
                                        start=(kc == 0), stop=(kc == 31))
                    return last
                tok_hm = P.emit("tensor", mk_h2,
                                waits=[hc_tok[sl]] + h1_acts + ph_free[pp], inc="PE")
                hc_users[sl].append(tok_hm)
                tok_ha = P.emit("scalar",
                                lambda e, mc=mc, ts=ts, pp=pp:
                                e.activation(u2sb[:, mc * 1024 + ts * 512:mc * 1024 + ts * 512 + 512],
                                             ph[pp][:, :], AF.Gelu, scale=scal["h2s"][:, mc:mc + 1]),
                                waits=[tok_hm, t_scal["h2s"], h1_mm_done], inc="ACT")
                ph_free[pp] = [tok_ha]
                h2_acts.append(tok_ha)
        h2_mm_done = ("PE", P.ctr["PE"])
        w3_toks = []
        for fc in range(32):
            w3_toks.append(P.dma("sync", big2[:, fc * 1024:(fc + 1) * 1024],
                                 w3b_dram[fc, :, :], waits=[h2_mm_done] + w3cast_toks[-1:], sem="w3s"))
        x1r_tok = [None, None]
        x1r_users = [[], []]
        py_free = [[[], []], [[], []]]
        yst_free = [None, None]
        out_free = [None, None]
        u2_users_prev = []
        for tt in range(8):
            sl = tt % 2
            gt = th * 8 + tt
            x1r_tok[sl] = P.dma("sync", x1r[sl][:, :], x1_dram[gt * 128:(gt + 1) * 128, :],
                                waits=x1r_users[sl] + [x1d_toks[-1]], sem=f"x1r{sl}")
            x1r_users[sl] = []
            mm_toks = []
            for eh in range(2):
                def mk_y(e, tt=tt, eh=eh, sl=sl):
                    last = None
                    for fc in range(32):
                        last = e.matmul(py[eh][sl][:, :],
                                        u2sb[:, fc * 1024 + tt * 128:fc * 1024 + tt * 128 + 128],
                                        big2[:, fc * 1024 + eh * 512:fc * 1024 + eh * 512 + 512],
                                        start=(fc == 0), stop=(fc == 31))
                    return last
                mm_toks.append(P.emit("tensor", mk_y,
                                      waits=w3_toks + h2_acts + py_free[eh][sl], inc="PE"))
            u2_users_prev += mm_toks
            sc_toks2 = []
            for eh in range(2):
                sc_toks2.append(P.emit("vector",
                                       lambda e, eh=eh, sl=sl:
                                       e.tensor_mul(yst[sl][:, eh * 512:eh * 512 + 512],
                                                    py[eh][sl][:, :], w3sb[:, eh * 512:eh * 512 + 512]),
                                       waits=[mm_toks[eh], t_w3b, yst_free[sl]], inc="DVE"))
                py_free[eh][sl] = [sc_toks2[eh]]
            tok_add = P.emit("vector",
                             lambda e, sl=sl: e.tensor_add(yst[sl][:, :], yst[sl][:, :], x1r[sl][:, :]),
                             waits=sc_toks2 + [x1r_tok[sl]], inc="DVE")
            x1r_users[sl].append(tok_add)
            bt = []
            for eh in range(2):
                bt.append(P.emit("vector",
                                 lambda e, eh=eh, sl=sl:
                                 e.bn_stats(st12b[sl][:, eh * 6:eh * 6 + 6], yst[sl][:, eh * 512:eh * 512 + 512]),
                                 waits=[tok_add], inc="DVE"))
            tok_ag2 = P.emit("vector", lambda e, sl=sl: e.bn_aggr(st2b[sl][:, :], st12b[sl][:, :]),
                             waits=bt, inc="DVE")
            tok_ve2 = P.emit("vector", lambda e, sl=sl: e.tensor_scalar_add(stdb[sl][:, :], st2b[sl][:, 1:2], EPS),
                             waits=[tok_ag2], inc="DVE")
            tok_sd2 = P.emit("scalar", lambda e, sl=sl: e.activation(stdb[sl][:, :], stdb[sl][:, :], AF.Sqrt),
                             waits=[tok_ve2], inc="ACT")
            tok_rs2 = P.emit("vector", lambda e, sl=sl: e.reciprocal(rstb[sl][:, :], stdb[sl][:, :]),
                             waits=[tok_sd2], inc="DVE")
            tok_o = P.emit("vector",
                           lambda e, sl=sl: e.tensor_scalar(outs[sl][:, :], yst[sl][:, :],
                                                            st2b[sl][:, 0:1], rstb[sl][:, :],
                                                            ALU.subtract, ALU.mult),
                           waits=[tok_rs2, out_free[sl]], inc="DVE")
            yst_free[sl] = tok_o
            tok_yd = P.dma("sync", y_out[gt * 128:(gt + 1) * 128, :], outs[sl][:, :], waits=[tok_o],
                           sem="ys")
            out_free[sl] = tok_yd
            y_dma.append(tok_yd)
    f2ctx.close()

    fence = [(k, v) for k, v in P.ctr.items()]
    P.emit("gpsimd", lambda e: e.nop(), waits=fence)

    sems = {}
    for s in list(P.ctr.keys()):
        sems[s] = ctx.enter_context(nc.semaphore(s))
    with nc.Block() as block:
        for en in ENGINES:
            def body(eng, en=en):
                water = {}
                for (engine, fn, waits, inc, amt) in P.steps:
                    if engine != en:
                        continue
                    for (s, v) in waits:
                        if water.get(s, 0) >= v:
                            continue
                        eng.wait_ge(sems[s], v)
                        water[s] = v
                    ins = fn(eng)
                    if inc is not None:
                        ins.then_inc(sems[inc], amt)
            getattr(block, en)(body)
    ctx.close()
    return nc


def make_inputs(x_full, Wq, Wk, Wv, Wo, W1, w1s, W2, w2s, H1, h1s, H2, h2s, W3, w3s):
    f32 = np.float32
    xt_b = []
    for b in range(2):
        xb = np.asarray(x_full[b], dtype=f32)
        xt_b.append(np.ascontiguousarray(
            xb.reshape(NTILE, 512, NCH, 128).transpose(0, 3, 2, 1).reshape(NTILE, 128, NCH * 512)))

    def quad_w(W, q):
        return np.ascontiguousarray(
            np.asarray(W[256 * q:256 * (q + 1), :], dtype=f32)
            .reshape(256, NCH, 128).transpose(2, 1, 0).reshape(128, NCH * 256))

    wo_h = np.ascontiguousarray(
        np.asarray(Wo, dtype=f32).T.reshape(NCH, 128, D).transpose(1, 0, 2).reshape(128, NCH * D))
    w1_h = np.ascontiguousarray(
        np.asarray(W1, dtype=f32).reshape(32, 128, NCH, 128).transpose(0, 3, 2, 1).reshape(32, 128, NCH * 128))
    w2_h = np.ascontiguousarray(
        np.asarray(W2, dtype=f32).reshape(32, 128, NCH, 128).transpose(0, 3, 2, 1).reshape(32, 128, NCH * 128))
    h1_h = np.ascontiguousarray(
        np.asarray(H1, dtype=f32).reshape(32, 128, 32, 128).transpose(0, 3, 2, 1).reshape(32, 128, 32 * 128))
    h2_h = np.ascontiguousarray(
        np.asarray(H2, dtype=f32).reshape(32, 128, 32, 128).transpose(0, 3, 2, 1).reshape(32, 128, 32 * 128))
    w3_h = np.ascontiguousarray(np.asarray(W3, dtype=f32).T.reshape(32, 128, D))
    scl = {k: np.ascontiguousarray(np.asarray(v, dtype=f32).reshape(32, 128).T)
           for k, v in (("w1s", w1s), ("w2s", w2s), ("h1s", h1s), ("h2s", h2s))}
    w3sb_h = np.ascontiguousarray(np.broadcast_to(np.asarray(w3s, dtype=f32)[None, :], (128, D)))

    j = np.arange(32, dtype=np.float64)
    inv_freq = (1.0 / 10000.0 ** (j / 32.0))
    pos = np.arange(T, dtype=np.float64)
    ang = pos[None, :] * inv_freq[:, None]
    cosr = np.cos(ang).astype(f32)
    sinr = np.sin(ang).astype(f32)
    cos_h = np.ascontiguousarray(np.tile(cosr, (4, 1)))
    sin_half = np.concatenate([-sinr, sinr], axis=0)
    sin_h = np.ascontiguousarray(np.tile(sin_half, (2, 1)))
    Rm = np.zeros((128, 128), dtype=BF)
    for blk in range(2):
        for d in range(32):
            Rm[blk * 64 + d, blk * 64 + 32 + d] = 1
            Rm[blk * 64 + 32 + d, blk * 64 + d] = 1
    ident_h = np.eye(128, dtype=BF)
    indic_h = np.zeros((128, 33), dtype=BF)
    indic_h[0:64, 0] = 1
    indic_h[64:128, 32] = 1
    ones1_h = np.ones((1, 128), dtype=f32)
    sel_lo_h = np.zeros((1, 128), dtype=f32)
    sel_lo_h[0, 0:64] = 1
    sel_hi_h = np.zeros((1, 128), dtype=f32)
    sel_hi_h[0, 64:128] = 1
    masks_h = np.zeros((128, 4, 2, 512), dtype=f32)
    jj = np.arange(512)
    ii = np.arange(128)
    for rb in range(4):
        pat = np.where(jj[None, :] < rb * 128 + ii[:, None], NEG, 0.0)
        masks_h[:, rb, 0, :] = pat
        masks_h[:, rb, 1, :] = pat
    masks_h = np.ascontiguousarray(masks_h.reshape(128, 4 * 1024))

    in_maps = []
    for core in range(8):
        b, q = core // 4, core % 4
        tok0 = q * NTOK
        off_h = np.asarray([[tok0 + (dc // 2) * 128 * T for dc in range(NCH)]], dtype=np.int32)
        im = {
            "xt_in": xt_b[b],
            "xres_in": np.ascontiguousarray(np.asarray(x_full[b][tok0:tok0 + NTOK, :], dtype=f32)),
            "wq_in": quad_w(Wq, q), "wk_in": quad_w(Wk, q), "wv_in": quad_w(Wv, q),
            "wo_in": wo_h, "w1_in": w1_h, "w2_in": w2_h,
            "h1_in": h1_h, "h2_in": h2_h, "w3_in": w3_h,
            "w1s_in": scl["w1s"], "w2s_in": scl["w2s"],
            "h1s_in": scl["h1s"], "h2s_in": scl["h2s"], "w3sb_in": w3sb_h,
            "cos_in": cos_h, "sin_in": sin_h,
            "rmat_in": Rm, "ident_in": ident_h, "indic_in": indic_h,
            "ones1_in": ones1_h, "sel_lo_in": sel_lo_h, "sel_hi_in": sel_hi_h,
            "masks_in": masks_h, "off_in": off_h,
        }
        in_maps.append(im)
    return in_maps


def kernel(L0_attn_x, L0_attn_Wq, L0_attn_Wk, L0_attn_Wv, L0_attn_Wo,
           L0_ffn_w1_w, L0_ffn_w1_s, L0_ffn_w2_w, L0_ffn_w2_s,
           L0_ffn_h1_w, L0_ffn_h1_s, L0_ffn_h2_w, L0_ffn_h2_s,
           L0_ffn_w3_w, L0_ffn_w3_s):
    x = np.asarray(L0_attn_x, dtype=np.float32)
    in_maps = make_inputs(x, L0_attn_Wq, L0_attn_Wk, L0_attn_Wv, L0_attn_Wo,
                          L0_ffn_w1_w, L0_ffn_w1_s, L0_ffn_w2_w, L0_ffn_w2_s,
                          L0_ffn_h1_w, L0_ffn_h1_s, L0_ffn_h2_w, L0_ffn_h2_s,
                          L0_ffn_w3_w, L0_ffn_w3_s)
    nc = build_program()
    res = run_bass_kernel_spmd(nc, in_maps, core_ids=list(range(8)))
    out = np.empty((2, T, D), dtype=np.float32)
    for core in range(8):
        b, q = core // 4, core % 4
        out[b, q * NTOK:(q + 1) * NTOK, :] = res.results[core]["y_out"]
    return out
